# version 12
# baseline (speedup 1.0000x reference)
import numpy as np
import ml_dtypes
import concourse.bass as bass
import concourse.mybir as mybir
from concourse.bass_utils import run_bass_kernel_spmd
from concourse.aot_env import allow_ici_all_to_all

F32 = mybir.dt.float32
BF16 = mybir.dt.bfloat16
AF = mybir.ActivationFunctionType
ALU = mybir.AluOpType
AX = mybir.AxisListType

B, T, D = 2, 4096, 2048
NCOR = 8
HPC = 2
NINST = 4
E = 16
TOK = 1024
NSTRIP = 16
NKC = 16
EPS = 1e-5
ALPHA = 1.702
SCL = float(1.0 / np.sqrt(128.0))
NEGINF = -1.0e30
RD = 1.0 / 2048.0

_PHASES = "all"


class S:
    def __init__(self, nc, name):
        self.h = nc.semaphore(name).__enter__()
        self.c = 0

    def inc(self, inst, n=1):
        inst.then_inc(self.h, n)
        self.c += n
        return self.c

    def wait(self, eng, c=None):
        eng.wait_ge(self.h, self.c if c is None else c)


def _phase_ge(a, b):
    order = ["a1", "a2", "a3", "ffn", "all"]
    return order.index(a) >= order.index(b)


def build_program(taps=()):
    taps = set(taps)
    ph = _PHASES
    nc = bass.Bass("TRN2", target_bir_lowering=False, debug=False)

    xT = nc.dram_tensor("xT", [B, D, T], F32, kind="ExternalInput").ap()
    xTo = nc.dram_tensor("xTo", [D, TOK], F32, kind="ExternalInput").ap()
    wqk = nc.dram_tensor("wqk", [2, HPC, 128, NKC, 128], F32, kind="ExternalInput").ap()
    wv = nc.dram_tensor("wv", [NKC, 128, HPC * 128], F32, kind="ExternalInput").ap()
    wo = nc.dram_tensor("wo", [16, 128, NKC, 128], F32, kind="ExternalInput").ap()
    w1a = nc.dram_tensor("w1a", [E, 16, 128, NKC, 128], F32, kind="ExternalInput").ap()
    w1b = nc.dram_tensor("w1b", [E, 16, 128, NKC, 128], F32, kind="ExternalInput").ap()
    w2 = nc.dram_tensor("w2", [E, 16, 128, NKC, 128], F32, kind="ExternalInput").ap()
    b1a = nc.dram_tensor("b1a", [128, E, 16], F32, kind="ExternalInput").ap()
    b1b = nc.dram_tensor("b1b", [128, E, 16], F32, kind="ExternalInput").ap()
    b2 = nc.dram_tensor("b2", [E, D], F32, kind="ExternalInput").ap()
    gw = nc.dram_tensor("gw", [NKC, 128, E], F32, kind="ExternalInput").ap()
    gb = nc.dram_tensor("gb", [1, E], F32, kind="ExternalInput").ap()
    nwc = nc.dram_tensor("nwc", [128, 16], F32, kind="ExternalInput").ap()
    nwr = nc.dram_tensor("nwr", [1, D], F32, kind="ExternalInput").ap()
    outT = nc.dram_tensor("outT", [D, TOK], F32, kind="ExternalOutput").ap()

    def inl(name, arr):
        return nc.inline_tensor(arr, name).ap()

    i_o128b = inl("i_o128b", np.ones((128, 1), ml_dtypes.bfloat16))
    i_or128f = inl("i_or128f", np.ones((1, 128), np.float32))
    i_idb = inl("i_idb", np.eye(128, dtype=ml_dtypes.bfloat16))
    i_idf = inl("i_idf", np.eye(128, dtype=np.float32))
    msk_np = np.zeros((4, 128, 512), np.float32)
    pc_ = np.arange(128)[:, None]
    jr_ = np.arange(512)[None, :]
    for i in range(4):
        msk_np[i] = np.where(128 * i + pc_ <= jr_, 0.0, NEGINF)
    i_masks = [inl(f"i_mask{i}", msk_np[i].astype(ml_dtypes.bfloat16)) for i in range(4)]
    oh_np = np.zeros((128, 16), np.float32)
    for h in range(4):
        oh_np[:, 4 * h + h] = 1.0
    i_oh4 = inl("i_oh4", oh_np.astype(ml_dtypes.bfloat16))
    oh16_np = np.zeros((16, 16 * 128), np.float32)
    for e in range(16):
        oh16_np[e, e * 128:(e + 1) * 128] = 1.0
    i_oh16 = inl("i_oh16", oh16_np.astype(ml_dtypes.bfloat16))
    i_eps = inl("i_eps", np.full((1, 1), EPS, np.float32))
    i_sclrow = inl("i_sclrow", np.full((1, 128), SCL, np.float32))
    i_o128f = inl("i_o128f", np.ones((128, 1), np.float32))

    k_scr = nc.dram_tensor("k_scr", [NINST, 128, T], BF16, kind="Internal").ap()
    v_scr = nc.dram_tensor("v_scr", [NINST, T, 128], BF16, kind="Internal").ap()
    zin_a = nc.dram_tensor("zin_a", [NCOR * 128, TOK], BF16, kind="Internal")
    zin_b = nc.dram_tensor("zin_b", [NCOR * 128, TOK], BF16, kind="Internal")
    zout_a = nc.dram_tensor("zout_a", [NCOR * 128, TOK], BF16, kind="Internal")
    zout_b = nc.dram_tensor("zout_b", [NCOR * 128, TOK], BF16, kind="Internal")
    x2_scr = nc.dram_tensor("x2_scr", [D, TOK], F32, kind="Internal").ap()

    tap_t = {}

    def mktap(name, shape, dt=F32):
        if name in taps:
            tap_t[name] = nc.dram_tensor("tap_" + name, shape, dt,
                                         kind="ExternalOutput").ap()

    mktap("q_s", [128, NINST, T])
    mktap("kvk", [NINST, 128, T])
    mktap("kvv", [NINST, T, 128])
    mktap("negB", [128, 4])
    mktap("z", [NCOR * 256, TOK])
    mktap("x2", [D, TOK])
    mktap("tt", [128, NKC, TOK])
    mktap("Gt", [16, TOK])
    mktap("delta", [128, NKC, TOK])

    sem = {n: S(nc, "sm_" + n) for n in
           ["ld0", "xb", "xfree", "sq", "stat", "statf", "row", "row2",
            "bc", "bcs", "col", "cole", "proj", "evq", "kvw", "sqr",
            "nmm", "nred", "vs", "as_", "ps", "finB", "ksq", "qsq",
            "a2ld", "a2ldf", "sc", "ex", "av", "zdone", "rr", "bc6",
            "rls", "zev", "zw", "cc", "phb",
            "g0", "g1", "g2", "g3", "g4", "g5", "g6", "g7", "g8", "g9",
            "h0", "h1", "h2", "h3", "h4", "h5", "h6", "h7", "h8", "h9",
            "f0", "f1", "f2", "f3", "f4", "f5", "f6", "f7", "f8", "f9",
            "f10", "f11", "fot", "fot1", "ldg", "xb1", "kvw1", "a2l1",
            "zw1", "g0b", "h0b", "h4b", "h7b", "pm",
            "wa0", "wa1", "wb0", "wb1", "wc0", "wc1"]}
    V, A_, P_ = sem["vs"], sem["as_"], sem["ps"]

    ctxs = []

    def sb(name, shape, dt):
        cm = nc.sbuf_tensor(name, shape, dt)
        t_ = cm.__enter__()
        ctxs.append(cm)
        return t_

    c_o128b = sb("s_o128b", [128, 1], BF16)
    c_or128f = sb("s_or128f", [1, 128], F32)
    c_idb = sb("s_idb", [128, 128], BF16)
    c_idf = sb("s_idf", [128, 128], F32)
    c_oh4 = sb("s_oh4", [128, 16], BF16)
    c_oh16 = sb("s_oh16", [16, 16 * 128], BF16)
    c_eps = sb("s_eps", [1, 1], F32)
    c_sclrow = sb("s_sclrow", [1, 128], F32)
    c_o128f = sb("s_o128f", [128, 1], F32)
    negB_s = sb("negB_s", [128, 4], F32)
    kn_max = sb("kn_max", [4, 1], F32)
    qn_max = sb("qn_max", [4, 1], F32)
    nrm_t = sb("nrm_t", [4, 4], F32)
    bn_row = sb("bn_row", [1, 4], F32)
    junk1 = sb("junk1", [1, 8], F32)

    pb = []
    for i in range(8):
        cm = nc.psum_tensor(f"pbank{i}", [128, 512], F32)
        pb.append(cm.__enter__())
        ctxs.append(cm)

    cnt = {}

    qkv_scope = []

    def sbq(name, shape, dt):
        cm = nc.sbuf_tensor(name, shape, dt)
        t_ = cm.__enter__()
        qkv_scope.append(cm)
        return t_

    c_masks = sbq("s_masks", [128, 4, 512], BF16)
    wqk_s = sbq("wqk_s", [128, 2, HPC, NKC, 128], BF16)
    wv_s = sbq("wv_s", [128, NKC, HPC * 128], BF16)
    q_s = sbq("q_sb", [128, NINST, T], BF16)
    wsum_qk = sbq("wsum_qk", [1, 2, HPC, 128], F32)
    wsum_v = sbq("wsum_v", [1, HPC * 128], F32)

    a1_scope = []

    def sba(name, shape, dt):
        cm = nc.sbuf_tensor(name, shape, dt)
        t_ = cm.__enter__()
        a1_scope.append(cm)
        return t_

    xb = [sba(f"xb{p}", [128, NKC, 512], BF16) for p in range(2)]
    sq_s = sba("sq_s", [128, NKC, 512], BF16)
    kst = [sba(f"kst{p}", [128, HPC, 512], BF16) for p in range(2)]
    vst = [sba(f"vst{p}", [128, 4, HPC * 128], BF16) for p in range(2)]
    rstb = [sba(f"rstb{p}", [128, 512], F32) for p in range(2)]
    rstqb = [sba(f"rstqb{p}", [128, 512], F32) for p in range(2)]
    rstc = [sba(f"rstc{p}", [128, 4], F32) for p in range(2)]
    rows = [sba(f"rows{p}", [1, 6, 512], F32) for p in range(2)]
    ksq_s = sba("ksq_s", [128, HPC, 512], BF16)
    qsq_s = sba("qsq_s", [128, HPC, 512], BF16)

    with nc.Block() as blk:
        @blk.sync
        def _(s):
            for dst, src in [(c_o128b, i_o128b), (c_or128f, i_or128f),
                             (c_idb, i_idb), (c_idf, i_idf),
                             (c_oh4, i_oh4), (c_oh16, i_oh16),
                             (c_eps, i_eps), (c_sclrow, i_sclrow),
                             (c_o128f, i_o128f)]:
                sem["ld0"].inc(s.dma_start(out=dst[:], in_=src), 16)
            for i in range(4):
                sem["ld0"].inc(s.dma_start(out=c_masks[:, i, :], in_=i_masks[i]), 16)

        @blk.gpsimd
        def _(g):
            sem["ldg"].inc(g.dma_start(
                out=wqk_s[:], in_=wqk.rearrange("p h kp kc o -> kp p h kc o")), 16)
            sem["ldg"].inc(g.dma_start(
                out=wv_s[:], in_=wv.rearrange("kc kp o -> kp kc o")), 16)

        @blk.tensor
        def _(t):
            sem["ld0"].wait(t)
            sem["ldg"].wait(t)
            mm = None
            first = True
            for pj in range(2):
                for hh in range(HPC):
                    for kc in range(NKC):
                        g_ = pj * HPC + hh
                        mm = t.matmul(pb[0][0:1, g_ * 128:(g_ + 1) * 128],
                                      c_o128b[:], wqk_s[:, pj, hh, kc, :],
                                      start=first,
                                      stop=(pj == 1 and hh == HPC - 1 and kc == NKC - 1))
                        first = False
            sem["proj"].inc(mm)
            for kc in range(NKC):
                mm = t.matmul(pb[1][0:1, 0:HPC * 128], c_o128b[:], wv_s[:, kc, :],
                              start=(kc == 0), stop=(kc == NKC - 1))
            sem["proj"].inc(mm)

        @blk.vector
        def _(v):
            sem["proj"].wait(v)
            V.inc(v.tensor_copy(wsum_qk[:].rearrange("a p h o -> a (p h o)"),
                                pb[0][0:1, 0:512]))
            V.inc(v.tensor_copy(wsum_v[:], pb[1][0:1, 0:HPC * 128]))
            V.inc(v.memset(kn_max[:], 0.0))
            V.inc(v.memset(qn_max[:], 0.0))
            sem["statf"].inc(v.engine_nop())
            sem["statf"].inc(v.engine_nop())

    PROJ0 = 2
    STATF0 = 2

    XBT = {0: 0, 1: 0}
    XBW = {}
    with nc.Block() as blk:
        @blk.gpsimd
        def _(g):
            for i in range(NSTRIP):
                b, si = divmod(i, 8)
                par = i % 2
                if i >= 2:
                    g.wait_ge(sem["proj"].h, PROJ0 + 8 * (i - 1))
                    g.wait_ge(sem["sq"].h, 4 * (i - 1))
                src = xT[b][:, 512 * si:512 * (si + 1)].rearrange(
                    "(kc kp) t -> kp kc t", kp=128)
                sxb = sem["xb"] if par == 0 else sem["xb1"]
                if i < 2:
                    for c4 in range(4):
                        g.wait_ge(sxb.h, XBT[par])
                        g.dma_start(out=xb[par][:, 4 * c4:4 * (c4 + 1), :],
                                    in_=src[:, 4 * c4:4 * (c4 + 1), :])\
                            .then_inc(sxb.h, 16)
                        XBT[par] += 16
                else:
                    g.wait_ge(sxb.h, XBT[par])
                    g.dma_start(out=xb[par][:], in_=src).then_inc(sxb.h, 16)
                    XBT[par] += 16
                XBW[i] = XBT[par]
            sem["xb"].c = XBT[0]
            sem["xb1"].c = XBT[1]

        @blk.tensor
        def _(t):
            for i in range(NSTRIP):
                b, si = divmod(i, 8)
                par = i % 2
                sxb = sem["xb"] if par == 0 else sem["xb1"]
                if i >= 2:
                    t.wait_ge(sxb.h, XBW[i])
                t.wait_ge(sem["statf"].h, STATF0 + 2 * i)
                for kc in range(NKC):
                    if i < 2 and kc % 4 == 0:
                        t.wait_ge(sxb.h, XBW[i] - 64 + 16 * (kc // 4 + 1))
                    mm = t.matmul(pb[0][0:1, 0:512], c_o128b[:], xb[par][:, kc, :],
                                  start=(kc == 0), stop=(kc == NKC - 1))
                mm.then_inc(sem["stat"].h, 1)
                if i < 2:
                    t.wait_ge(sxb.h, XBW[i])

                groups = [("q", 0), ("q", 1), ("k", 0), ("k", 1),
                          ("v", 0), ("v", 1), ("v", 2), ("v", 3)]

                def emit16(gidx):
                    kind, idx = groups[gidx]
                    bank = pb[5 + gidx % 3]
                    if i > 0 or gidx >= 3:
                        if gidx >= 3:
                            pg = 8 * i + (gidx - 3) + 1
                        else:
                            pg = 8 * (i - 1) + (gidx + 5) + 1
                        t.wait_ge(sem["evq"].h, pg)
                    if kind == "q":
                        for kc in range(NKC):
                            t.matmul(bank[:, 0:512], wqk_s[:, 0, idx, kc, :],
                                     xb[par][:, kc, :], start=(kc == 0), stop=False)
                    elif kind == "k":
                        for kc in range(NKC):
                            t.matmul(bank[:, 0:512], wqk_s[:, 1, idx, kc, :],
                                     xb[par][:, kc, :], start=(kc == 0), stop=False)
                    else:
                        for kc in range(NKC):
                            t.matmul(bank[:, 0:HPC * 128],
                                     xb[par][:, kc, 128 * idx:128 * (idx + 1)],
                                     wv_s[:, kc, :], start=(kc == 0), stop=False)

                def emit_corr(gidx):
                    kind, idx = groups[gidx]
                    bank = pb[5 + gidx % 3]
                    nm = rows[par][0:1, 1, :]
                    if kind == "q":
                        mm = t.matmul(bank[:, 0:512], wsum_qk[0:1, 0, idx, :], nm,
                                      start=False, stop=True)
                    elif kind == "k":
                        mm = t.matmul(bank[:, 0:512], wsum_qk[0:1, 1, idx, :], nm,
                                      start=False, stop=True)
                    else:
                        mm = t.matmul(bank[:, 0:HPC * 128],
                                      nm[:, 128 * idx:128 * (idx + 1)], wsum_v[:],
                                      start=False, stop=True)
                    mm.then_inc(sem["proj"].h, 1)
                    return mm

                emit16(0)
                emit16(1)
                t.wait_ge(sem["statf"].h, STATF0 + 2 * i)
                t.wait_ge(sem["sq"].h, 4 * (i + 1))
                for kc in range(NKC):
                    mm = t.matmul(pb[1][0:1, 0:512], c_o128b[:], sq_s[:, kc, :],
                                  start=(kc == 0), stop=(kc == NKC - 1))
                mm.then_inc(sem["stat"].h, 1)
                emit16(2)
                t.wait_ge(sem["row2"].h, i + 1)
                if i > 0:
                    t.wait_ge(sem["bcs"].h, 2 * (i - 1) + 1)
                t.matmul(pb[2][:, 0:512], c_or128f[:], rows[par][0:1, 5, :],
                         start=True, stop=True).then_inc(sem["bc"].h, 1)
                if i > 0:
                    t.wait_ge(sem["bcs"].h, 2 * (i - 1) + 2)
                t.matmul(pb[3][:, 0:512], c_sclrow[:], rows[par][0:1, 5, :],
                         start=True, stop=True).then_inc(sem["bc"].h, 1)
                if i > 0:
                    t.wait_ge(sem["nred"].h, 2 * i)
                for st in range(4):
                    mm = t.matmul(pb[4][:, st:st + 1],
                                  rows[par][0:1, 5, 128 * st:128 * (st + 1)],
                                  c_or128f[0:1, 0:1], start=(st == 0), stop=(st == 3))
                mm.then_inc(sem["col"].h, 1)
                for gidx in range(3, 8):
                    emit_corr(gidx - 3)
                    emit16(gidx)
                for gidx in range(5, 8):
                    mm = emit_corr(gidx)
                t.wait_ge(sem["cole"].h, i + 1)
                t.wait_ge(sem["ksq"].h, i + 1)
                for hh in range(HPC):
                    mm = t.matmul(pb[4][0:4, 0:512],
                                  c_oh4[:, 4 * (2 * b + hh):4 * (2 * b + hh) + 4],
                                  ksq_s[:, hh, :], start=(hh == 0), stop=(hh == HPC - 1))
                mm.then_inc(sem["nmm"].h, 1)
                t.wait_ge(sem["nred"].h, 2 * i + 1)
                t.wait_ge(sem["qsq"].h, i + 1)
                for hh in range(HPC):
                    mm = t.matmul(pb[4][0:4, 0:512],
                                  c_oh4[:, 4 * (2 * b + hh):4 * (2 * b + hh) + 4],
                                  qsq_s[:, hh, :], start=(hh == 0), stop=(hh == HPC - 1))
                mm.then_inc(sem["nmm"].h, 1)
            sem["stat"].c = 2 * NSTRIP
            sem["bc"].c = 2 * NSTRIP
            sem["col"].c = NSTRIP
            sem["nmm"].c = 2 * NSTRIP
            sem["proj"].c = PROJ0 + 8 * NSTRIP

        @blk.scalar
        def _(a):
            for i in range(NSTRIP):
                par = i % 2
                sxb = sem["xb"] if par == 0 else sem["xb1"]
                if i >= 2:
                    a.wait_ge(sxb.h, XBW[i])
                if i > 0:
                    a.wait_ge(sem["stat"].h, 2 * i)
                for j in range(4):
                    if i < 2:
                        a.wait_ge(sxb.h, XBW[i] - 64 + 16 * (j + 1))
                    ins = a.activation(sq_s[:, 4 * j:4 * (j + 1), :],
                                       xb[par][:, 4 * j:4 * (j + 1), :], AF.Square)
                    ins.then_inc(sem["sq"].h, 1)
                a.wait_ge(sem["statf"].h, STATF0 + 2 * i + 2)
                a.activation(rows[par][0:1, 4, :], rows[par][0:1, 3, :],
                             AF.Sqrt, bias=c_eps[0:1, 0:1]).then_inc(sem["sqr"].h, 1)
                a.wait_ge(sem["bc"].h, 2 * i + 1)
                if i > 1:
                    a.wait_ge(sem["evq"].h, 8 * (i - 1))
                a.copy(rstb[par][:], pb[2][:, 0:512]).then_inc(sem["bcs"].h, 1)
                a.wait_ge(sem["bc"].h, 2 * i + 2)
                a.copy(rstqb[par][:], pb[3][:, 0:512]).then_inc(sem["bcs"].h, 1)
                a.wait_ge(sem["col"].h, i + 1)
                a.copy(rstc[par][:], pb[4][:, 0:4]).then_inc(sem["cole"].h, 1)
                if i > 0:
                    a.wait_ge(sem["nmm"].h, 2 * i)
                a.wait_ge(sem["evq"].h, 8 * i + 4)
                a.activation(ksq_s[:], kst[par][:], AF.Square).then_inc(sem["ksq"].h, 1)
                b_, si = divmod(i, 8)
                insts = [2 * b_ + 0, 2 * b_ + 1]
                for hh in range(HPC):
                    ins = a.activation(qsq_s[:, hh, :],
                                       q_s[:, insts[hh], 512 * si:512 * (si + 1)],
                                       AF.Square)
                ins.then_inc(sem["qsq"].h, 1)
            sem["sq"].c = 4 * NSTRIP
            sem["sqr"].c = NSTRIP
            sem["bcs"].c = 2 * NSTRIP
            sem["cole"].c = NSTRIP
            sem["ksq"].c = NSTRIP
            sem["qsq"].c = NSTRIP

        @blk.vector
        def _(v):
            vc = V.c
            for i in range(NSTRIP):
                b_, si = divmod(i, 8)
                par = i % 2
                if i > 1:
                    v.wait_ge((sem["kvw"] if par == 0 else sem["kvw1"]).h,
                              160 * (i // 2))
                v.wait_ge(sem["stat"].h, 2 * i + 1)
                if i > 1:
                    v.wait_ge(sem["col"].h, i - 1)
                vc = V.inc(v.tensor_scalar(rows[par][0:1, 0, :], pb[0][0:1, 0:512],
                                           RD, None, ALU.mult))
                v.wait_ge(V.h, vc)
                ins = v.tensor_scalar(rows[par][0:1, 1, :], rows[par][0:1, 0, :],
                                      -1.0, None, ALU.mult)
                ins.then_inc(sem["row"].h, 1)
                vc = V.inc(v.tensor_mul(rows[par][0:1, 2, :], rows[par][0:1, 0, :],
                                        rows[par][0:1, 0, :]))
                v.wait_ge(sem["stat"].h, 2 * i + 2)
                v.wait_ge(V.h, vc)
                ins = v.scalar_tensor_tensor(rows[par][0:1, 3, :], pb[1][0:1, 0:512],
                                             RD, rows[par][0:1, 2, :],
                                             ALU.mult, ALU.subtract)
                ins.then_inc(sem["statf"].h, 2)
                v.wait_ge(sem["sqr"].h, i + 1)
                ins = v.reciprocal(rows[par][0:1, 5, :], rows[par][0:1, 4, :])
                ins.then_inc(sem["row2"].h, 1)
                insts = [2 * b_ + 0, 2 * b_ + 1]
                for g_ in range(8):
                    v.wait_ge(sem["proj"].h, PROJ0 + 8 * i + g_ + 1)
                    bank = pb[5 + g_ % 3]
                    if g_ < 2:
                        v.wait_ge(sem["bcs"].h, 2 * i + 2)
                        ins = v.tensor_mul(q_s[:, insts[g_], 512 * si:512 * (si + 1)],
                                           bank[:, 0:512], rstqb[par][:])
                    elif g_ < 4:
                        ins = v.tensor_mul(kst[par][:, g_ - 2, :],
                                           bank[:, 0:512], rstb[par][:])
                    else:
                        st = g_ - 4
                        v.wait_ge(sem["cole"].h, i + 1)
                        ins = v.tensor_scalar(vst[par][:, st, :], bank[:, 0:HPC * 128],
                                              rstc[par][:, st:st + 1], None, ALU.mult)
                    ins.then_inc(sem["evq"].h, 1)
                v.wait_ge(sem["nmm"].h, 2 * i + 1)
                vc = V.inc(v.reduce_max(nrm_t[0:4, 0:1], pb[4][0:4, 0:512], AX.X))
                v.wait_ge(V.h, vc)
                if i > 0:
                    v.wait_ge(sem["nred"].h, 2 * i - 1)
                ins = v.tensor_max(kn_max[:], kn_max[:], nrm_t[0:4, 0:1])
                ins.then_inc(sem["nred"].h, 1)
                v.wait_ge(sem["nmm"].h, 2 * i + 2)
                vc = V.inc(v.reduce_max(nrm_t[0:4, 1:2], pb[4][0:4, 0:512], AX.X))
                v.wait_ge(V.h, vc)
                if i > 0:
                    v.wait_ge(sem["nred"].h, 2 * i)
                ins = v.tensor_max(qn_max[:], qn_max[:], nrm_t[0:4, 1:2])
                ins.then_inc(sem["nred"].h, 1)
            sem["row"].c = NSTRIP
            sem["row2"].c = NSTRIP
            sem["statf"].c = STATF0 + 2 * NSTRIP
            sem["evq"].c = 8 * NSTRIP
            sem["nred"].c = 2 * NSTRIP

        @blk.sync
        def _(s):
            for i in range(NSTRIP):
                b_, si = divmod(i, 8)
                par = i % 2
                skv = sem["kvw"] if par == 0 else sem["kvw1"]
                s.wait_ge(skv.h, 160 * (i // 2))
                s.wait_ge(sem["evq"].h, 8 * i + 8)
                for hh in range(HPC):
                    inst = 2 * b_ + hh
                    s.dma_start(out=k_scr[inst, :, 512 * si:512 * (si + 1)],
                                in_=kst[par][:, hh, :]).then_inc(skv.h, 16)
                for st in range(4):
                    for hh in range(HPC):
                        inst = 2 * b_ + hh
                        s.dma_start(
                            out=v_scr[inst, 512 * si + 128 * st:512 * si + 128 * (st + 1), :],
                            in_=vst[par][:, st, 128 * hh:128 * (hh + 1)],
                        ).then_inc(skv.h, 16)
            sem["kvw"].c = 160 * (NSTRIP // 2)
            sem["kvw1"].c = 160 * (NSTRIP // 2)

    with nc.Block() as blk:
        @blk.vector
        def _(v):
            v.wait_ge(sem["nred"].h, 2 * NSTRIP)
            V.inc(v.tensor_mul(nrm_t[0:4, 2:3], kn_max[:], qn_max[:]))

        @blk.scalar
        def _(a):
            V.wait(a)
            A_.inc(a.activation(nrm_t[0:4, 3:4], nrm_t[0:4, 2:3], AF.Sqrt))

        @blk.vector
        def _(v):
            A_.wait(v)
            V.inc(v.tensor_scalar(bn_row[0:1, 0:4].tensor[0:4, 0:1] if False else nrm_t[0:4, 2:3],
                                  nrm_t[0:4, 3:4], -1.0, None, ALU.mult))

        @blk.tensor
        def _(t):
            V.wait(t)
            t.transpose(pb[4][0:1, 0:4], nrm_t[0:4, 2:3], c_idf[0:4, 0:4])\
                .then_inc(sem["finB"].h, 1)

        @blk.scalar
        def _(a):
            a.wait_ge(sem["finB"].h, 1)
            A_.inc(a.copy(bn_row[:], pb[4][0:1, 0:4]))

        @blk.tensor
        def _(t):
            A_.wait(t)
            for ih in range(4):
                mm = t.matmul(pb[4][:, ih:ih + 1], c_or128f[:],
                              bn_row[0:1, ih:ih + 1], start=(ih == 0), stop=(ih == 3))
            mm.then_inc(sem["finB"].h, 1)

        @blk.scalar
        def _(a):
            a.wait_ge(sem["finB"].h, 2)
            a.copy(negB_s[:], pb[4][:, 0:4]).then_inc(sem["finB"].h, 1)
        sem["finB"].c = 3

    if "q_s" in tap_t or "kvk" in tap_t or "negB" in tap_t:
        with nc.Block() as blk:
            @blk.vector
            def _(v):
                if "q_s" in tap_t:
                    v.wait_ge(sem["evq"].h, 8 * NSTRIP)
                if "negB" in tap_t:
                    v.wait_ge(sem["finB"].h, 3)
                V.inc(v.engine_nop())

            @blk.gpsimd
            def _(s):
                V.wait(s)
                n = 0
                if "q_s" in tap_t:
                    qf = tap_t["q_s"]
                    s.dma_start(out=qf, in_=q_s[:]).then_inc(sem["g9"].h, 16)
                    n += 16
                if "negB" in tap_t:
                    s.dma_start(out=tap_t["negB"], in_=negB_s[:]).then_inc(sem["g9"].h, 16)
                    n += 16
                if "kvk" in tap_t:
                    s.wait_ge(sem["kvw"].h, 160 * NSTRIP)
                    s.dma_start(out=tap_t["kvk"], in_=k_scr).then_inc(sem["g9"].h, 16)
                    s.dma_start(out=tap_t["kvv"], in_=v_scr).then_inc(sem["g9"].h, 16)
                    n += 32
                s.wait_ge(sem["g9"].h, n)
                sem["g9"].c = n

    if ph == "a1":
        return nc, tap_t

    for cm in reversed(a1_scope):
        cm.__exit__(None, None, None)
    a2_scope = []

    def sb2(name, shape, dt):
        cm = nc.sbuf_tensor(name, shape, dt)
        t_ = cm.__enter__()
        a2_scope.append(cm)
        return t_

    kh = [sb2(f"kh{p}", [128, 32, 128], BF16) for p in range(2)]
    vh = [sb2(f"vh{p}", [128, 32, 128], BF16) for p in range(2)]
    pt_s = sb2("pt_s", [128, 32, 512], BF16)
    zst = [sb2(f"zst{p}", [128, 512], BF16) for p in range(2)]
    rl_s = [sb2(f"rl{p}", [128, 512], F32) for p in range(2)]
    rr_s = sb2("rr_s", [1, 2, 512], F32)

    NS = [4 * (g + 1) for g in range(8)]
    IHSEQ = [0, 2, 1, 3]
    SCnt = []
    tot = 0
    for n in range(32):
        SCnt.append(tot)
        tot += NS[n % 8]
    TOTST = tot

    with nc.Block() as blk:
        @blk.sync
        def _(s):
            s.wait_ge(sem["kvw"].h, 160 * (NSTRIP // 2))
            s.wait_ge(sem["kvw1"].h, 160 * (NSTRIP // 2))
            s.wait_ge(sem["proj"].h, PROJ0 + 8 * NSTRIP)
            s.wait_ge(sem["sq"].h, 4 * NSTRIP)
            s.wait_ge(sem["stat"].h, 2 * NSTRIP)
            s.wait_ge(sem["evq"].h, 8 * NSTRIP)
            s.wait_ge(sem["nred"].h, 2 * NSTRIP)
            s.wait_ge(sem["finB"].h, 3)
            for pos in range(NINST):
                inst = IHSEQ[pos]
                par = pos % 2
                sld = sem["a2ld"] if par == 0 else sem["a2l1"]
                s.wait_ge(sld.h, 144 * (pos // 2))
                if pos >= 2:
                    s.wait_ge(sem["a2ldf"].h, pos - 1)
                s.dma_start(out=kh[par][:].rearrange("p st d -> p (st d)"),
                            in_=k_scr[inst]).then_inc(sld.h, 16)
                for j in range(8):
                    src = v_scr[inst, 512 * j:512 * (j + 1), :].rearrange(
                        "(st sp) d -> sp st d", sp=128)
                    s.dma_start(out=vh[par][:, 4 * j:4 * (j + 1), :], in_=src)\
                        .then_inc(sld.h, 16)
            sem["a2ld"].c = 16 * 9 * (NINST // 2)
            sem["a2l1"].c = 16 * 9 * (NINST // 2)

        @blk.tensor
        def _(t):
            t.wait_ge(sem["evq"].h, 8 * NSTRIP)
            slot_av = [0] * 32
            SCB = [0, 1, 7]
            bank_ex = [0, 0, 0]
            scn = 0
            avn = 0
            pend_bc = None
            for n in range(32):
                pos, g = divmod(n, 8)
                inst = IHSEQ[pos]
                par = pos % 2
                ns = NS[g]
                if g == 0:
                    t.wait_ge((sem["a2ld"] if par == 0 else sem["a2l1"]).h,
                              16 * 9 * (pos // 2 + 1))
                if n >= 2:
                    t.wait_ge(sem["zev"].h, n - 1)
                    t.wait_ge(sem["rr"].h, n - 1)
                zb = pb[2 + n % 2]
                lb = pb[4 + n % 2]
                for s_ in range(ns):
                    scb = pb[SCB[s_ % 3]]
                    if bank_ex[s_ % 3]:
                        t.wait_ge(sem["ex"].h, bank_ex[s_ % 3])
                    mm = t.matmul(scb[:, 0:512], kh[par][:, s_, :],
                                  q_s[:, inst, 512 * g:512 * (g + 1)],
                                  start=True, stop=(s_ < 4 * g))
                    if s_ >= 4 * g:
                        mm = t.matmul(scb[:, 0:512], c_idb[:],
                                      c_masks[:, s_ - 4 * g, :], start=False, stop=True)
                    scn += 1
                    mm.then_inc(sem["sc"].h, 1)
                    bank_ex[s_ % 3] = scn
                    if s_ > 0:
                        sp = s_ - 1
                        t.wait_ge(sem["ex"].h, SCnt[n] + sp + 1)
                        t.matmul(zb[:, 0:512], vh[par][:, sp, :], pt_s[:, sp, :],
                                 start=(sp == 0), stop=False)
                        mm = t.matmul(lb[0:1, 0:512], c_o128b[:], pt_s[:, sp, :],
                                      start=(sp == 0), stop=False)
                        avn += 1
                        mm.then_inc(sem["av"].h, 1)
                        slot_av[sp] = avn
                    if pend_bc is not None and s_ == min(ns - 2, 9):
                        np_, = pend_bc
                        t.wait_ge(sem["rr"].h, np_ + 1)
                        if np_ >= 2:
                            t.wait_ge(sem["rls"].h, np_ - 1)
                        t.matmul(pb[6][:, 0:512], c_or128f[:],
                                 rr_s[0:1, np_ % 2, :], start=True, stop=True)\
                            .then_inc(sem["bc6"].h, 1)
                        pend_bc = None
                sp = ns - 1
                t.wait_ge(sem["ex"].h, SCnt[n] + sp + 1)
                mm = t.matmul(zb[:, 0:512], vh[par][:, sp, :], pt_s[:, sp, :],
                              start=(sp == 0), stop=True)
                if g == 7:
                    mm.then_inc(sem["a2ldf"].h, 1)
                mm2 = t.matmul(lb[0:1, 0:512], c_o128b[:], pt_s[:, sp, :],
                               start=(sp == 0), stop=True)
                avn += 1
                mm2.then_inc(sem["av"].h, 1)
                slot_av[sp] = avn
                pend_bc = (n,)
            np_, = pend_bc
            t.wait_ge(sem["rr"].h, np_ + 1)
            t.wait_ge(sem["rls"].h, np_ - 1)
            t.matmul(pb[6][:, 0:512], c_or128f[:], rr_s[0:1, np_ % 2, :],
                     start=True, stop=True).then_inc(sem["bc6"].h, 1)
            sem["sc"].c = TOTST
            sem["av"].c = TOTST
            sem["bc6"].c = 32
            sem["a2ldf"].c = NINST
            cnt["a2_slot_av"] = slot_av

        @blk.scalar
        def _(a):
            a.wait_ge(sem["finB"].h, 3)
            slot_av = [0] * 32
            scn = 0
            avn = 0
            for n in range(32):
                pos, g = divmod(n, 8)
                inst = IHSEQ[pos]
                ns = NS[g]
                for s_ in range(ns):
                    scb = pb[[0, 1, 7][s_ % 3]]
                    scn += 1
                    a.wait_ge(sem["sc"].h, scn)
                    if slot_av[s_]:
                        a.wait_ge(sem["av"].h, slot_av[s_])
                    a.activation(pt_s[:, s_, :], scb[:, 0:512], AF.Exp,
                                 bias=negB_s[:, inst:inst + 1]).then_inc(sem["ex"].h, 1)
                for s_ in range(ns):
                    avn += 1
                    slot_av[s_] = avn
                if n >= 1:
                    m_ = n - 1
                    a.wait_ge(sem["bc6"].h, m_ + 1)
                    if m_ >= 2:
                        a.wait_ge(sem["zev"].h, m_ - 1)
                    a.copy(rl_s[m_ % 2][:], pb[6][:, 0:512]).then_inc(sem["rls"].h, 1)
            a.wait_ge(sem["bc6"].h, 32)
            a.copy(rl_s[31 % 2][:], pb[6][:, 0:512]).then_inc(sem["rls"].h, 1)
            sem["ex"].c = TOTST
            sem["rls"].c = 32

        @blk.vector
        def _(v):
            for n in range(32):
                ih, g = divmod(n, 8)
                ns = NS[g]
                v.wait_ge(sem["av"].h, SCnt[n] + ns)
                ins = v.reciprocal(rr_s[0:1, n % 2, :], pb[4 + n % 2][0:1, 0:512])
                ins.then_inc(sem["rr"].h, 1)
                v.wait_ge(sem["rls"].h, n + 1)
                if n >= 2:
                    v.wait_ge((sem["zw"] if n % 2 == 0 else sem["zw1"]).h,
                              16 * (n // 2))
                ins = v.tensor_mul(zst[n % 2][:], pb[2 + n % 2][:, 0:512],
                                   rl_s[n % 2][:])
                ins.then_inc(sem["zev"].h, 1)
            sem["rr"].c = 32
            sem["zev"].c = 32

        @blk.gpsimd
        def _(g_):
            for n in range(32):
                pos, g = divmod(n, 8)
                inst = IHSEQ[pos]
                b_ = inst // 2
                hh = inst % 2
                peer = b_ * 4 + g // 2
                zin_h = zin_a if hh == 0 else zin_b
                g_.wait_ge(sem["zev"].h, n + 1)
                dst = zin_h.ap()[peer * 128:(peer + 1) * 128,
                                 (g % 2) * 512:(g % 2) * 512 + 512]
                szw = sem["zw"] if n % 2 == 0 else sem["zw1"]
                g_.wait_ge(szw.h, 16 * (n // 2))
                g_.dma_start(out=dst, in_=zst[n % 2][:]).then_inc(szw.h, 16)
                if n == 15:
                    g_.wait_ge(sem["zw"].h, 16 * 8)
                    g_.wait_ge(sem["zw1"].h, 16 * 8)
                    with allow_ici_all_to_all():
                        g_.collective_compute(
                            "AllToAll", ALU.bypass,
                            replica_groups=[list(range(NCOR))],
                            ins=[zin_a.ap().opt()], outs=[zout_a.ap().opt()],
                        ).then_inc(sem["cc"].h, 1)
            sem["zw"].c = 16 * 16
            sem["zw1"].c = 16 * 16
            g_.wait_ge(sem["zw"].h, 16 * 16)
            g_.wait_ge(sem["zw1"].h, 16 * 16)
            with allow_ici_all_to_all():
                g_.collective_compute(
                    "AllToAll", ALU.bypass,
                    replica_groups=[list(range(NCOR))],
                    ins=[zin_b.ap().opt()], outs=[zout_b.ap().opt()],
                ).then_inc(sem["cc"].h, 1)
            sem["cc"].c = 2

    if "z" in tap_t:
        with nc.Block() as blk:
            @blk.gpsimd
            def _(s):
                s.wait_ge(sem["cc"].h, 2)
                s.dma_start(out=tap_t["z"][0:1024, :], in_=zout_a.ap())\
                    .then_inc(sem["g8"].h, 16)
                s.wait_ge(sem["g8"].h, 16)
                s.dma_start(out=tap_t["z"][1024:2048, :], in_=zout_b.ap())\
                    .then_inc(sem["g8"].h, 16)
                s.wait_ge(sem["g8"].h, 32)
                sem["g8"].c = 32

    if ph == "a2":
        return nc, tap_t

    for cm in reversed(a2_scope):
        cm.__exit__(None, None, None)
    for cm in reversed(qkv_scope):
        cm.__exit__(None, None, None)

    post_scope = []

    def sbp(name, shape, dt):
        cm = nc.sbuf_tensor(name, shape, dt)
        t_ = cm.__enter__()
        post_scope.append(cm)
        return t_

    t_s = sbp("t_sb", [128, NKC, TOK], BF16)
    delta = sbp("delta", [128, NKC, TOK], F32)
    Gt_s = sbp("Gt_s", [16, TOK], BF16)
    negm2r = sbp("negm2r", [1, TOK], F32)
    r2_r = sbp("r2_r", [1, TOK], F32)

    a3o_scope = []

    def sbo(name, shape, dt):
        cm = nc.sbuf_tensor(name, shape, dt)
        t_ = cm.__enter__()
        a3o_scope.append(cm)
        return t_

    ao_s = sbo("ao_s", [128, 16, TOK], BF16)

    p1_scope = []

    def sb1(name, shape, dt):
        cm = nc.sbuf_tensor(name, shape, dt)
        t_ = cm.__enter__()
        p1_scope.append(cm)
        return t_

    zf_s = sb1("zf_s", [128, 16, TOK], BF16)
    wo_b = [sb1(f"wo_b{p}", [128, NKC, 128], BF16) for p in range(2)]
    aosq = sb1("aosq", [128, TOK], BF16)

    with nc.Block() as blk:
        @blk.sync
        def _(s):
            s.wait_ge(sem["zev"].h, 32)
            s.wait_ge(sem["cc"].h, 1)
            s.dma_start(out=zf_s[:, 0:16:2, :],
                        in_=zout_a.ap().rearrange("(dc kp) t -> kp dc t", kp=128))                .then_inc(sem["g1"].h, 16)
            s.wait_ge(sem["cc"].h, 2)
            s.wait_ge(sem["g1"].h, 16)
            s.dma_start(out=zf_s[:, 1:16:2, :],
                        in_=zout_b.ap().rearrange("(dc kp) t -> kp dc t", kp=128))                .then_inc(sem["g1"].h, 16)
            sem["g1"].c += 32

        @blk.gpsimd
        def _(g):
            g.wait_ge(sem["zev"].h, 32)
            for oc in range(16):
                if oc >= 2:
                    g.wait_ge(sem["g2"].h, 2 * (oc - 1))
                sg0 = sem["g0"] if oc % 2 == 0 else sem["g0b"]
                g.wait_ge(sg0.h, 16 * (oc // 2))
                g.dma_start(out=wo_b[oc % 2][:], in_=wo[oc]).then_inc(sg0.h, 16)
            sem["g0"].c += 16 * 8
            sem["g0b"].c += 16 * 8

        @blk.tensor
        def _(t):
            t.wait_ge(sem["g1"].h, sem["g1"].c - 16)

            def stats_mm(oc):
                t.wait_ge(sem["g3"].h, 2 * (oc + 1))
                for tg in range(2):
                    t.matmul(pb[4 + tg][0:1, 0:512], c_o128b[:],
                             ao_s[:, oc, 512 * tg:512 * (tg + 1)],
                             start=(oc == 0), stop=(oc == 15))
                t.wait_ge(sem["g4"].h, oc + 1)
                for tg in range(2):
                    mm = t.matmul(pb[6 + tg][0:1, 0:512], c_o128b[:],
                                  aosq[:, 512 * tg:512 * (tg + 1)],
                                  start=(oc == 0), stop=(oc == 15))
                mm.then_inc(sem["g5"].h, 1)

            for oc in range(16):
                t.wait_ge((sem["g0"] if oc % 2 == 0 else sem["g0b"]).h,
                          16 * (oc // 2 + 1))
                for tg in range(2):
                    bank = pb[2 * (oc % 2) + tg]
                    if oc >= 2:
                        t.wait_ge(sem["g3"].h, 2 * (oc - 1))
                    korder = list(range(0, NKC, 2)) + list(range(1, NKC, 2))
                    for ki, kc in enumerate(korder):
                        if ki == NKC // 2 and oc == 0 and tg == 0:
                            t.wait_ge(sem["g1"].h, sem["g1"].c)
                        mm = t.matmul(bank[:, 0:512], wo_b[oc % 2][:, kc, :],
                                      zf_s[:, kc, 512 * tg:512 * (tg + 1)],
                                      start=(ki == 0), stop=(ki == NKC - 1))
                    mm.then_inc(sem["g2"].h, 1)
                if oc >= 1:
                    stats_mm(oc - 1)
            stats_mm(15)
            sem["g2"].c += 32
            sem["g5"].c += 16

        @blk.scalar
        def _(a):
            for oc in range(16):
                for tg in range(2):
                    a.wait_ge(sem["g2"].h, 2 * oc + tg + 1)
                    a.copy(ao_s[:, oc, 512 * tg:512 * (tg + 1)],
                           pb[2 * (oc % 2) + tg][:, 0:512]).then_inc(sem["g3"].h, 1)
                if oc >= 1:
                    a.wait_ge(sem["g5"].h, oc)
                a.wait_ge(sem["g3"].h, 2 * (oc + 1))
                a.activation(aosq[:], ao_s[:, oc, :], AF.Square).then_inc(sem["g4"].h, 1)
            sem["g3"].c += 32
            sem["g4"].c += 16

    for cm in reversed(p1_scope):
        cm.__exit__(None, None, None)

    p2_scope = []

    def sb2b(name, shape, dt):
        cm = nc.sbuf_tensor(name, shape, dt)
        t_ = cm.__enter__()
        p2_scope.append(cm)
        return t_

    rowsB = sb2b("rowsB", [1, 5, TOK], F32)
    xo_b = [sb2b(f"xo_b{p}", [128, TOK], F32) for p in range(2)]
    xp_b = [sb2b(f"xp_b{p}", [128, TOK], F32) for p in range(2)]
    x2_b = [sb2b(f"x2_b{p}", [128, TOK], F32) for p in range(2)]
    tmpf = sb2b("tmpf", [128, TOK], F32)
    tmpg = sb2b("tmpg", [128, TOK], F32)
    sqb = sb2b("sqb", [128, TOK], BF16)
    Rb_s = sb2b("Rb_s", [128, TOK], F32)
    Cb_s = sb2b("Cb_s", [128, TOK], F32)
    nwc_s = sb2b("nwc_s", [128, 16], F32)

    with nc.Block() as blk:
        @blk.vector
        def _(v):
            v.wait_ge(sem["g5"].h, sem["g5"].c)
            vc = V.c
            for tg in range(2):
                vc = V.inc(v.tensor_scalar(rowsB[0:1, 0, 512 * tg:512 * (tg + 1)],
                                           pb[4 + tg][0:1, 0:512], RD, None, ALU.mult))
            v.wait_ge(V.h, vc)
            vc = V.inc(v.tensor_scalar(rowsB[0:1, 1, :], rowsB[0:1, 0, :],
                                       -1.0, None, ALU.mult))
            vc = V.inc(v.tensor_mul(rowsB[0:1, 2, :], rowsB[0:1, 0, :],
                                    rowsB[0:1, 0, :]))
            v.wait_ge(V.h, vc)
            for tg in range(2):
                ins = v.scalar_tensor_tensor(rowsB[0:1, 3, 512 * tg:512 * (tg + 1)],
                                             pb[6 + tg][0:1, 0:512], RD,
                                             rowsB[0:1, 2, 512 * tg:512 * (tg + 1)],
                                             ALU.mult, ALU.subtract)
            ins.then_inc(sem["g6"].h, 1)
            sem["g6"].c += 1

        @blk.scalar
        def _(a):
            a.wait_ge(sem["g6"].h, sem["g6"].c)
            a.activation(rowsB[0:1, 4, :], rowsB[0:1, 3, :], AF.Sqrt,
                         bias=c_eps[0:1, 0:1]).then_inc(sem["as_"].h, 1)
            sem["as_"].c += 1

        @blk.vector
        def _(v):
            A_.wait(v)
            vc = V.inc(v.reciprocal(rowsB[0:1, 2, :], rowsB[0:1, 4, :]))
            v.wait_ge(V.h, vc)
            ins = v.tensor_mul(rowsB[0:1, 3, :], rowsB[0:1, 1, :], rowsB[0:1, 2, :])
            ins.then_inc(sem["g6"].h, 1)
            sem["g6"].c += 1

    with nc.Block() as blk:
        @blk.tensor
        def _(t):
            t.wait_ge(sem["g6"].h, sem["g6"].c)
            t.wait_ge(sem["g3"].h, sem["g3"].c)
            for tg in range(2):
                t.matmul(pb[0 + tg][:, 0:512], c_or128f[:],
                         rowsB[0:1, 2, 512 * tg:512 * (tg + 1)],
                         start=True, stop=True).then_inc(sem["g7"].h, 1)
            for tg in range(2):
                t.matmul(pb[2 + tg][:, 0:512], c_or128f[:],
                         rowsB[0:1, 3, 512 * tg:512 * (tg + 1)],
                         start=True, stop=True).then_inc(sem["g7"].h, 1)
            sem["g7"].c += 4

        @blk.scalar
        def _(a):
            a.wait_ge(sem["g7"].h, sem["g7"].c)
            for tg in range(2):
                a.copy(Rb_s[:, 512 * tg:512 * (tg + 1)], pb[0 + tg][:, 0:512])
            for tg in range(2):
                ins = a.copy(Cb_s[:, 512 * tg:512 * (tg + 1)], pb[2 + tg][:, 0:512])
            ins.then_inc(sem["as_"].h, 1)
            sem["as_"].c += 1

        @blk.sync
        def _(s):
            s.wait_ge(sem["g5"].h, sem["g5"].c)
            s.wait_ge(sem["g1"].h, sem["g1"].c)
            s.dma_start(out=nwc_s[:], in_=nwc).then_inc(sem["g1"].h, 16)
            sem["g1"].c += 16

    A3A = sem["as_"].c
    with nc.Block() as blk:
        @blk.sync
        def _(s):
            for dc in range(0, 16, 2):
                if dc >= 2:
                    s.wait_ge(sem["h1"].h, dc - 1)
                s.wait_ge(sem["h0"].h, 16 * (dc // 2))
                s.dma_start(out=xo_b[0][:], in_=xTo[128 * dc:128 * (dc + 1), :])                    .then_inc(sem["h0"].h, 16)
            sem["h0"].c += 128
            sem["h0b"].c += 128

        @blk.gpsimd
        def _(g):
            PM1 = sem["pm"].c
            tpg = [tmpf, tmpg]
            for dc in range(16):
                if dc >= 2:
                    g.wait_ge(sem["h1"].h, dc - 1)
                g.tensor_mul(tpg[dc % 2][:], ao_s[:, dc, :], Rb_s[:])\
                    .then_inc(sem["pm"].h, 1)
                if dc >= 1:
                    g.wait_ge(sem["h1"].h, dc)
                    sh4 = sem["h4"] if (dc - 1) % 2 == 0 else sem["h4b"]
                    g.wait_ge(sh4.h, 16 * ((dc - 1) // 2))
                    g.dma_start(out=x2_scr[128 * (dc - 1):128 * dc, :],
                                in_=x2_b[(dc - 1) % 2][:]).then_inc(sh4.h, 16)
            g.wait_ge(sem["h1"].h, 16)
            g.dma_start(out=x2_scr[128 * 15:128 * 16, :],
                        in_=x2_b[1][:]).then_inc(sem["h4b"].h, 16)
            sem["pm"].c = PM1 + 16
            sem["h4"].c += 128
            sem["h4b"].c += 128

        @blk.vector
        def _(v):
            v.wait_ge(sem["as_"].h, A3A)
            PM1 = sem["pm"].c - 16
            tpg = [tmpf, tmpg]
            for dc in range(16):
                v.wait_ge((sem["h0"] if dc % 2 == 0 else sem["h0b"]).h,
                          16 * (dc // 2 + 1))
                if dc >= 2:
                    v.wait_ge(sem["h2"].h, dc - 1)
                    v.wait_ge(sem["h5"].h, dc - 1)
                    v.wait_ge((sem["h4"] if dc % 2 == 0 else sem["h4b"]).h,
                              16 * (dc // 2))
                vc = V.inc(v.tensor_add(x2_b[dc % 2][:], xo_b[dc % 2][:], Cb_s[:]))
                v.wait_ge(sem["pm"].h, PM1 + dc + 1)
                v.wait_ge(V.h, vc)
                ins = v.tensor_add(x2_b[dc % 2][:], x2_b[dc % 2][:],
                                   tpg[dc % 2][:])
                ins.then_inc(sem["h1"].h, 1)
            sem["h1"].c += 16

        @blk.scalar
        def _(a):
            for dc in range(16):
                if dc % 2 == 1:
                    if dc >= 2:
                        a.wait_ge(sem["h1"].h, dc - 1)
                    a.wait_ge(sem["h0b"].h, 16 * (dc // 2))
                    a.dma_start(out=xo_b[1][:],
                                in_=xTo[128 * dc:128 * (dc + 1), :])\
                        .then_inc(sem["h0b"].h, 16)
                a.wait_ge(sem["h1"].h, dc + 1)
                if dc >= 1:
                    a.wait_ge(sem["h5"].h, dc)
                a.activation(sqb[:], x2_b[dc % 2][:], AF.Square).then_inc(sem["h2"].h, 1)
            a.wait_ge(sem["h4b"].h, 16)
            a.dma_start(out=xp_b[1][:], in_=x2_scr[128:256, :])                .then_inc(sem["h7b"].h, 16)
            sem["h2"].c += 16

        @blk.sync
        def _(s):
            s.wait_ge(sem["h4"].h, 16)
            s.dma_start(out=xp_b[0][:], in_=x2_scr[0:128, :])                .then_inc(sem["h7"].h, 16)

        @blk.tensor
        def _(t):
            t.wait_ge(sem["g6"].h, sem["g6"].c)
            for dc in range(16):
                t.wait_ge(sem["h1"].h, dc + 1)
                for tg in range(2):
                    t.matmul(pb[4 + tg][0:1, 0:512], c_o128f[:],
                             x2_b[dc % 2][:, 512 * tg:512 * (tg + 1)],
                             start=(dc == 0), stop=(dc == 15))
                t.wait_ge(sem["h2"].h, dc + 1)
                for tg in range(2):
                    mm = t.matmul(pb[6 + tg][0:1, 0:512], c_o128b[:],
                                  sqb[:, 512 * tg:512 * (tg + 1)],
                                  start=(dc == 0), stop=(dc == 15))
                mm.then_inc(sem["h5"].h, 1)
            sem["h5"].c += 16

    with nc.Block() as blk:
        @blk.vector
        def _(v):
            v.wait_ge(sem["h5"].h, sem["h5"].c)
            vc = V.c
            for tg in range(2):
                vc = V.inc(v.tensor_scalar(rowsB[0:1, 0, 512 * tg:512 * (tg + 1)],
                                           pb[4 + tg][0:1, 0:512], RD, None, ALU.mult))
            v.wait_ge(V.h, vc)
            vc = V.inc(v.tensor_scalar(negm2r[:], rowsB[0:1, 0, :], -1.0,
                                       None, ALU.mult))
            vc = V.inc(v.tensor_mul(rowsB[0:1, 2, :], rowsB[0:1, 0, :],
                                    rowsB[0:1, 0, :]))
            v.wait_ge(V.h, vc)
            for tg in range(2):
                ins = v.scalar_tensor_tensor(rowsB[0:1, 3, 512 * tg:512 * (tg + 1)],
                                             pb[6 + tg][0:1, 0:512], RD,
                                             rowsB[0:1, 2, 512 * tg:512 * (tg + 1)],
                                             ALU.mult, ALU.subtract)
            ins.then_inc(sem["h6"].h, 1)
            sem["h6"].c += 1

        @blk.scalar
        def _(a):
            a.wait_ge(sem["h6"].h, sem["h6"].c)
            a.activation(rowsB[0:1, 4, :], rowsB[0:1, 3, :], AF.Sqrt,
                         bias=c_eps[0:1, 0:1]).then_inc(sem["as_"].h, 1)
            sem["as_"].c += 1

        @blk.vector
        def _(v):
            A_.wait(v)
            vc = V.inc(v.reciprocal(r2_r[:], rowsB[0:1, 4, :]))
            v.wait_ge(V.h, vc)
            vc = V.inc(v.tensor_mul(rowsB[0:1, 2, :], r2_r[:], r2_r[:]))
            v.wait_ge(V.h, vc)
            ins = v.tensor_mul(rowsB[0:1, 2, :], rowsB[0:1, 2, :], rowsB[0:1, 3, :])
            ins.then_inc(sem["h6"].h, 1)
            sem["h6"].c += 1

        @blk.scalar
        def _(a):
            a.wait_ge(sem["h6"].h, sem["h6"].c)
            a.activation(rowsB[0:1, 2, :], rowsB[0:1, 2, :], AF.Sqrt,
                         bias=c_eps[0:1, 0:1]).then_inc(sem["as_"].h, 1)
            sem["as_"].c += 1

        @blk.vector
        def _(v):
            A_.wait(v)
            vc = V.inc(v.reciprocal(rowsB[0:1, 2, :], rowsB[0:1, 2, :]))
            v.wait_ge(V.h, vc)
            vc = V.inc(v.tensor_mul(rowsB[0:1, 4, :], r2_r[:], rowsB[0:1, 2, :]))
            v.wait_ge(V.h, vc)
            ins = v.tensor_mul(rowsB[0:1, 3, :], negm2r[:], rowsB[0:1, 4, :])
            ins.then_inc(sem["h6"].h, 1)
            sem["h6"].c += 1

    PM3 = sem["pm"].c
    A3T = sem["as_"].c + 1
    with nc.Block() as blk:
        @blk.tensor
        def _(t):
            t.wait_ge(sem["h6"].h, sem["h6"].c)
            t.wait_ge(sem["h1"].h, sem["h1"].c)
            for tg in range(2):
                t.matmul(pb[0 + tg][:, 0:512], c_or128f[:],
                         rowsB[0:1, 4, 512 * tg:512 * (tg + 1)],
                         start=True, stop=True).then_inc(sem["g7"].h, 1)
            for tg in range(2):
                t.matmul(pb[2 + tg][:, 0:512], c_or128f[:],
                         rowsB[0:1, 3, 512 * tg:512 * (tg + 1)],
                         start=True, stop=True).then_inc(sem["g7"].h, 1)
            sem["g7"].c += 4

        @blk.scalar
        def _(a):
            a.wait_ge(sem["g7"].h, sem["g7"].c)
            for tg in range(2):
                a.copy(Rb_s[:, 512 * tg:512 * (tg + 1)], pb[0 + tg][:, 0:512])
            for tg in range(2):
                ins = a.copy(Cb_s[:, 512 * tg:512 * (tg + 1)], pb[2 + tg][:, 0:512])
            ins.then_inc(sem["as_"].h, 1)
            sem["as_"].c += 1
            for dc in range(3, 16, 2):
                a.wait_ge(sem["h4b"].h, 16 * (dc // 2 + 1))
                a.wait_ge(sem["pm"].h, PM3 + dc - 1)
                a.wait_ge(sem["h7b"].h, 16 * (dc // 2))
                a.dma_start(out=xp_b[1][:],
                            in_=x2_scr[128 * dc:128 * (dc + 1), :])                    .then_inc(sem["h7b"].h, 16)
            sem["h7b"].c += 128

        @blk.sync
        def _(s):
            for dc in range(2, 16, 2):
                s.wait_ge(sem["h4"].h, 16 * (dc // 2 + 1))
                s.wait_ge(sem["pm"].h, PM3 + dc - 1)
                s.wait_ge(sem["h7"].h, 16 * (dc // 2))
                s.dma_start(out=xp_b[0][:],
                            in_=x2_scr[128 * dc:128 * (dc + 1), :])                    .then_inc(sem["h7"].h, 16)
            sem["h7"].c += 128

        @blk.gpsimd
        def _(g):
            H9_0p = sem["h9"].c
            tpg = [tmpf, tmpg]
            g.wait_ge(sem["as_"].h, A3T)
            for dc in range(16):
                g.wait_ge((sem["h7"] if dc % 2 == 0 else sem["h7b"]).h,
                          16 * (dc // 2 + 1))
                if dc >= 2:
                    g.wait_ge(sem["h9"].h, H9_0p + 2 * (dc - 1))
                g.tensor_mul(tpg[dc % 2][:], xp_b[dc % 2][:], Rb_s[:])\
                    .then_inc(sem["pm"].h, 1)
            sem["pm"].c = PM3 + 16

        @blk.vector
        def _(v):
            v.wait_ge(sem["as_"].h, sem["as_"].c)
            v.wait_ge(sem["g1"].h, sem["g1"].c)
            H9_0 = sem["h9"].c
            tpg = [tmpf, tmpg]
            for dc in range(16):
                v.wait_ge(sem["pm"].h, PM3 + dc + 1)
                vc = V.inc(v.tensor_add(tpg[dc % 2][:], tpg[dc % 2][:], Cb_s[:]))
                v.wait_ge(V.h, vc)
                for tg in range(2):
                    ins = v.tensor_scalar(
                        t_s[:, dc, 512 * tg:512 * (tg + 1)],
                        tpg[dc % 2][:, 512 * tg:512 * (tg + 1)],
                        nwc_s[:, dc:dc + 1], None, ALU.mult)
                    ins.then_inc(sem["h9"].h, 1)
            sem["h9"].c += 32

    if "x2" in tap_t or "tt" in tap_t:
        with nc.Block() as blk:
            @blk.sync
            def _(s):
                if "x2" in tap_t:
                    s.wait_ge(sem["h4"].h, sem["h4"].c)
                    s.wait_ge(sem["h4b"].h, sem["h4b"].c)
                    s.dma_start(out=tap_t["x2"], in_=x2_scr).then_inc(sem["g9"].h, 16)
                    sem["g9"].c += 16

            @blk.gpsimd
            def _(g):
                if "tt" in tap_t:
                    g.wait_ge(sem["h9"].h, sem["h9"].c)
                    g.dma_start(out=tap_t["tt"], in_=t_s[:]).then_inc(sem["phb"].h, 16)
                    sem["phb"].c += 16

    if ph == "a3":
        return nc, tap_t

    for cm in reversed(p2_scope):
        cm.__exit__(None, None, None)
    for cm in reversed(a3o_scope):
        cm.__exit__(None, None, None)

    ffn_scope = []

    def sbf(name, shape, dt):
        cm = nc.sbuf_tensor(name, shape, dt)
        t_ = cm.__enter__()
        ffn_scope.append(cm)
        return t_

    finpre_scope = []

    def sbq(name, shape, dt):
        cm = nc.sbuf_tensor(name, shape, dt)
        t_ = cm.__enter__()
        finpre_scope.append(cm)
        return t_

    xo2 = [sbq(f"xo2_{p}", [128, TOK], F32) for p in range(2)]
    R2b = sbq("R2b", [128, TOK], F32)
    C2b = sbq("C2b", [128, TOK], F32)
    sqc = sbq("sqc", [128, TOK], BF16)

    act_s = sbf("act_s", [128, NKC, TOK], BF16)
    w1a_b = [sbf(f"w1a_b{p}", [128, NKC, 128], BF16) for p in range(2)]
    w1b_b = [sbf(f"w1b_b{p}", [128, NKC, 128], BF16) for p in range(2)]
    w2_b = [sbf(f"w2_b{p}", [128, NKC, 128], BF16) for p in range(2)]
    gbc = sbf("gbc", [128, 2, 512], BF16)
    b1a_s = sbf("b1a_s", [128, E * 16], F32)
    b1b_s = sbf("b1b_s", [128, E * 16], F32)
    b1b1_s = sbf("b1b1_s", [128, E * 16], F32)
    b2_s = sbf("b2_s", [16, D], BF16)
    gw_s = sbf("gw_s", [128, NKC, E], BF16)
    gb_s = sbf("gb_s", [1, E], F32)
    lg_s = sbf("lg_s", [128, 16], F32)
    junkg = sbf("junkg", [128, 512], F32)
    nmax_s = sbf("nmax_s", [128, 1], F32)
    eexp_s = sbf("eexp_s", [128, 16], F32)
    rkg_s = sbf("rkg_s", [128, 16], F32)
    rke_s = sbf("rke_s", [128, 16], F32)
    msk_s = sbf("msk_s", [128, 16], F32)
    den_s = sbf("den_s", [128, 2], F32)
    at_b = [sbf(f"at_b{p}", [128, 512], BF16) for p in range(2)]
    sa_b = [sbf(f"sa_b{p}", [128, 512], BF16) for p in range(2)]
    c1_b = [sbf(f"c1_b{p}", [128, 512], BF16) for p in range(2)]
    m2_b = [sbf(f"m2_b{p}", [128, 512], BF16) for p in range(2)]

    T_DONE = sem["h9"].c

    with nc.Block() as blk:
        @blk.sync
        def _(s):
            s.wait_ge(sem["h9"].h, T_DONE)
            s.dma_start(out=b1a_s[:], in_=b1a.rearrange("p e o -> p (e o)"))                .then_inc(sem["f1"].h, 16)
            s.dma_start(out=b1b_s[:], in_=b1b.rearrange("p e o -> p (e o)"))                .then_inc(sem["f1"].h, 16)
            s.dma_start(out=gb_s[:], in_=gb).then_inc(sem["f1"].h, 16)
            sem["f1"].c += 48

        @blk.gpsimd
        def _(g):
            g.wait_ge(sem["h9"].h, T_DONE)
            g.wait_ge(sem["ldg"].h, sem["ldg"].c)
            g.dma_start(out=b2_s[:], in_=b2).then_inc(sem["ldg"].h, 16)
            g.dma_start(out=gw_s[:], in_=gw.rearrange("kc kp e -> kp kc e"))                .then_inc(sem["ldg"].h, 16)
            sem["ldg"].c += 32

        @blk.vector
        def _(v):
            v.wait_ge(sem["f1"].h, sem["f1"].c)
            v.wait_ge(sem["ldg"].h, sem["ldg"].c)
            ins = v.tensor_scalar(b1b1_s[:], b1b_s[:], 1.0, None, ALU.add)
            ins.then_inc(sem["f3"].h, 1)
            sem["f3"].c += 1

    if ph == "smalls":
        with nc.Block() as blk:
            @blk.gpsimd
            def _(g):
                g.wait_ge(sem["f1"].h, sem["f1"].c)
                g.wait_ge(sem["f3"].h, sem["f3"].c)
                if "Gt" in tap_t:
                    g.dma_start(out=tap_t["Gt"], in_=Gt_s[:]).then_inc(sem["g9"].h, 16)
                    sem["g9"].c += 16
        return nc, tap_t

    F2_0 = sem["f2"].c
    F3_0 = sem["f3"].c
    F4_0 = sem["f4"].c
    A0 = sem["as_"].c
    V0 = sem["vs"].c
    with nc.Block() as blk:
        @blk.tensor
        def _(t):
            t.wait_ge(sem["f1"].h, sem["f1"].c)
            t.wait_ge(sem["ldg"].h, sem["ldg"].c)
            t.wait_ge(sem["h9"].h, T_DONE)
            t.wait_ge(sem["f3"].h, F3_0)

            def transpose_G(tt9):
                t.wait_ge(sem["f3"].h, F3_0 + 2 * tt9 + 2)
                if tt9 >= 1:
                    t.wait_ge(sem["f4"].h, F4_0 + 2 * tt9)
                t.transpose(pb[5][0:16, 0:128], lg_s[:], c_idf[:])\
                    .then_inc(sem["f4"].h, 1)

            for tt in range(8):
                bank = pb[6 + tt % 2]
                if tt >= 2:
                    t.wait_ge(sem["f3"].h, F3_0 + 2 * (tt - 2) + 1)
                for kc in range(NKC):
                    t.matmul(bank[:, 0:16], t_s[:, kc, 128 * tt:128 * (tt + 1)],
                             gw_s[:, kc, :], start=(kc == 0), stop=False)
                t.matmul(bank[:, 0:16], c_or128f[:], gb_s[:],
                         start=False, stop=True).then_inc(sem["f2"].h, 1)
                if tt >= 2:
                    transpose_G(tt - 2)
            transpose_G(6)
            transpose_G(7)
            sem["f2"].c += 8

        @blk.vector
        def _(v):
            for tt in range(8):
                v.wait_ge(sem["f2"].h, F2_0 + tt + 1)
                if tt >= 1:
                    v.wait_ge(sem["f4"].h, F4_0 + 2 * (tt - 1) + 1)
                v.tensor_copy(lg_s[:], pb[6 + tt % 2][:, 0:16])                    .then_inc(sem["f3"].h, 1)
                v.wait_ge(sem["f3"].h, F3_0 + 2 * tt + 1)
                V.inc(v.reduce_max(nmax_s[:], lg_s[:], AX.X, negate=True))
                for e_ in range(16):
                    v.tensor_scalar(junkg[:, 16 * e_:16 * (e_ + 1)], lg_s[:],
                                    lg_s[:, e_:e_ + 1], None,
                                    ALU.is_gt, ALU.add,
                                    accum_out=rkg_s[:, e_:e_ + 1])
                v.memset(rke_s[:, 0:1], 0.0)
                for e_ in range(1, 16):
                    v.tensor_scalar(junkg[:, 256 + 16 * e_:256 + 16 * e_ + e_],
                                    lg_s[:, 0:e_],
                                    lg_s[:, e_:e_ + 1], None,
                                    ALU.is_equal, ALU.add,
                                    accum_out=rke_s[:, e_:e_ + 1])
                V.inc(v.engine_nop())
                v.wait_ge(V.h, V0 + 8 * tt + 2)
                V.inc(v.tensor_add(rkg_s[:], rkg_s[:], rke_s[:]))
                v.wait_ge(V.h, V0 + 8 * tt + 3)
                V.inc(v.tensor_scalar(msk_s[:], rkg_s[:], 4.0, None, ALU.is_lt))
                v.wait_ge(sem["as_"].h, A0 + tt + 1)
                v.wait_ge(V.h, V0 + 8 * tt + 4)
                V.inc(v.tensor_mul(eexp_s[:], eexp_s[:], msk_s[:]))
                v.wait_ge(V.h, V0 + 8 * tt + 5)
                V.inc(v.tensor_scalar(msk_s[:], eexp_s[:], 1.0, None,
                                      ALU.mult, ALU.add,
                                      accum_out=den_s[:, 0:1]))
                v.wait_ge(V.h, V0 + 8 * tt + 6)
                V.inc(v.reciprocal(den_s[:, 1:2], den_s[:, 0:1]))
                v.wait_ge(V.h, V0 + 8 * tt + 7)
                V.inc(v.tensor_scalar(lg_s[:], eexp_s[:], den_s[:, 1:2], None,
                                      ALU.mult))
                v.engine_nop().then_inc(sem["f3"].h, 1)
            sem["f3"].c += 16
            sem["vs"].c = V0 + 64

        @blk.scalar
        def _(a):
            def evac_G(k9):
                a.wait_ge(sem["f4"].h, F4_0 + 2 * k9 + 1)
                a.copy(Gt_s[0:16, 128 * k9:128 * (k9 + 1)], pb[5][0:16, 0:128])                    .then_inc(sem["f4"].h, 1)

            for tt in range(8):
                if tt >= 1:
                    evac_G(tt - 1)
                a.wait_ge(sem["vs"].h, V0 + 8 * tt + 1)
                if tt >= 1:
                    a.wait_ge(sem["f3"].h, F3_0 + 2 * (tt - 1) + 2)
                a.activation(eexp_s[:], lg_s[:], AF.Exp, bias=nmax_s[:])                    .then_inc(sem["as_"].h, 1)
            evac_G(7)
            sem["as_"].c += 8
            sem["f4"].c = F4_0 + 16

    GT_DONE = sem["f4"].c

    F5_0 = sem["f5"].c
    ZD_0 = sem["zdone"].c
    with nc.Block() as blk:
        @blk.tensor
        def _(t):
            t.wait_ge(sem["f4"].h, GT_DONE)
            for i_ in range(32):
                dc, tg = divmod(i_, 2)
                if i_ >= 2:
                    t.wait_ge(sem["zdone"].h, ZD_0 + i_ - 1)
                t.matmul(pb[4 + i_ % 2][:, 0:512],
                         b2_s[0:16, 128 * dc:128 * (dc + 1)],
                         Gt_s[0:16, 512 * tg:512 * (tg + 1)],
                         start=True, stop=True).then_inc(sem["f5"].h, 1)
            sem["f5"].c = F5_0 + 32

        @blk.scalar
        def _(a):
            for i_ in range(32):
                dc, tg = divmod(i_, 2)
                a.wait_ge(sem["f5"].h, F5_0 + i_ + 1)
                a.copy(delta[:, dc, 512 * tg:512 * (tg + 1)],
                       pb[4 + i_ % 2][:, 0:512]).then_inc(sem["zdone"].h, 1)
            sem["zdone"].c = ZD_0 + 32
    DI_DONE = sem["zdone"].c

    if ph == "gate":
        if "Gt" in tap_t:
            with nc.Block() as blk:
                @blk.gpsimd
                def _(g):
                    g.wait_ge(sem["f4"].h, GT_DONE)
                    g.wait_ge(sem["zdone"].h, DI_DONE)
                    g.dma_start(out=tap_t["Gt"], in_=Gt_s[:]).then_inc(sem["g9"].h, 16)
                    sem["g9"].c += 16
        return nc, tap_t

    G8_0 = sem["g8"].c
    F2G = sem["f2"].c
    F7_END = sem["f7"].c + 2 * 32 * E
    H6_PRE = sem["h6"].c
    H1_PRE = sem["h1"].c
    H2_PRE = sem["h2"].c
    H5_PRE = sem["h5"].c
    H0_PRE = sem["h0"].c
    H0b_PRE = sem["h0b"].c
    ASP = sem["as_"].c
    G7P = sem["g7"].c
    f0_at = {}
    with nc.Block() as blk:
        @blk.gpsimd
        def _(p):
            p.wait_ge(sem["h9"].h, T_DONE)

            wslot = {"wa0": sem["wa0"], "wa1": sem["wa1"],
                     "wb0": sem["wb0"], "wb1": sem["wb1"],
                     "wc0": sem["wc0"], "wc1": sem["wc1"]}

            def load_w1(u):
                e9, oc9 = divmod(u, 16)
                for nm, wten, wbuf in (("a", w1a, w1a_b), ("b", w1b, w1b_b)):
                    sl = wslot["w" + nm + str(oc9 % 2)]
                    p.wait_ge(sl.h, sl.c)
                    if u >= 2:
                        p.wait_ge(sem["f6"].h, 2 * (u - 2) + 2)
                    p.dma_start(out=wbuf[oc9 % 2][:], in_=wten[e9, oc9])\
                        .then_inc(sl.h, 16)
                    sl.c += 16
                    f0_at[("w1" + nm, u)] = sl.c

            def load_w2(v9):
                e9, dc9 = divmod(v9, 16)
                sl = wslot["wc" + str(dc9 % 2)]
                p.wait_ge(sl.h, sl.c)
                if v9 >= 2:
                    p.wait_ge(sem["f10"].h, 2 * (v9 - 2) + 2)
                p.dma_start(out=w2_b[dc9 % 2][:], in_=w2[e9, dc9])\
                    .then_inc(sl.h, 16)
                sl.c += 16
                f0_at[("w2", v9)] = sl.c

            load_w1(0)
            load_w1(1)
            for e_ in range(E):
                for oc in range(16):
                    u = 16 * e_ + oc
                    if u + 2 < 16 * E:
                        load_w1(u + 2)
                    if oc < 2:
                        load_w2(16 * e_ + oc)
                    for tg in range(2):
                        j_ = 2 * u + tg
                        p.wait_ge(sem["f2"].h, F2G + 2 * j_ + 2)
                        p.wait_ge(sem["g8"].h, G8_0 + 4 * e_ + 2 * (tg + 1))
                        if e_ >= 1:
                            p.wait_ge(sem["f10"].h, 32 * e_)
                        p.tensor_mul(act_s[:, oc, 512 * tg:512 * (tg + 1)],
                                     m2_b[j_ % 2][:], gbc[:, tg, :])\
                            .then_inc(sem["f9"].h, 1)
                for dc in range(2, 16):
                    load_w2(16 * e_ + dc)
            sem["f9"].c += 32 * E

        @blk.tensor
        def _(t):
            t.wait_ge(sem["f4"].h, GT_DONE)
            t.wait_ge(sem["zdone"].h, DI_DONE)
            for e_ in range(E):
                if e_ >= 1:
                    t.wait_ge(sem["f11"].h, 32 * e_)
                for tg in range(2):
                    gidx = 2 * e_ + tg
                    if gidx >= 1:
                        t.wait_ge(sem["g8"].h, G8_0 + 2 * gidx)
                    t.matmul(pb[5][:, 0:512],
                             c_oh16[0:16, 128 * e_:128 * (e_ + 1)],
                             Gt_s[0:16, 512 * tg:512 * (tg + 1)],
                             start=True, stop=True).then_inc(sem["g8"].h, 1)
                for oc in range(16):
                    u = 16 * e_ + oc
                    t.wait_ge(sem["wa" + str(oc % 2)].h, f0_at[("w1a", u)])
                    t.wait_ge(sem["wb" + str(oc % 2)].h, f0_at[("w1b", u)])
                    for tg in range(2):
                        j_ = 2 * u + tg
                        hA = pb[0 + j_ % 2]
                        hB = pb[2 + j_ % 2]
                        if j_ >= 2:
                            t.wait_ge(sem["f7"].h, 2 * (j_ - 2) + 2)
                        for kc in range(NKC):
                            t.matmul(hA[:, 0:512], w1a_b[oc % 2][:, kc, :],
                                     t_s[:, kc, 512 * tg:512 * (tg + 1)],
                                     start=(kc == 0), stop=(kc == NKC - 1))
                        for kc in range(NKC):
                            mm = t.matmul(hB[:, 0:512], w1b_b[oc % 2][:, kc, :],
                                          t_s[:, kc, 512 * tg:512 * (tg + 1)],
                                          start=(kc == 0), stop=(kc == NKC - 1))
                        mm.then_inc(sem["f6"].h, 1)
                t.wait_ge(sem["f9"].h, 32 * (e_ + 1))
                if e_ == E - 1:
                    t.wait_ge(sem["f7"].h, F7_END)
                    for tg in range(2):
                        t.matmul(pb[0 + tg][:, 0:512], c_or128f[:],
                                 r2_r[0:1, 512 * tg:512 * (tg + 1)],
                                 start=True, stop=True).then_inc(sem["g7"].h, 1)
                    for tg in range(2):
                        t.matmul(pb[2 + tg][:, 0:512], c_or128f[:],
                                 negm2r[0:1, 512 * tg:512 * (tg + 1)],
                                 start=True, stop=True).then_inc(sem["g7"].h, 1)
                for dc in range(16):
                    v9 = 16 * e_ + dc
                    t.wait_ge(sem["wc" + str(dc % 2)].h, f0_at[("w2", v9)])
                    for tg in range(2):
                        y_j = 2 * v9 + tg
                        yb = pb[4 + y_j % 2]
                        if y_j >= 2:
                            t.wait_ge(sem["f11"].h, y_j - 1)
                        for ic in range(NKC):
                            mm = t.matmul(yb[:, 0:512], w2_b[dc % 2][:, ic, :],
                                          act_s[:, ic, 512 * tg:512 * (tg + 1)],
                                          start=(ic == 0), stop=(ic == NKC - 1))
                        mm.then_inc(sem["f10"].h, 1)
            sem["g8"].c = G8_0 + 2 * E
            sem["f6"].c += 2 * 16 * E
            sem["f10"].c += 2 * 16 * E
            sem["g7"].c += 4

        @blk.scalar
        def _(a):
            for e_ in range(E):
                for tg in range(2):
                    gidx = 2 * e_ + tg
                    a.wait_ge(sem["g8"].h, G8_0 + 2 * gidx + 1)
                    if e_ >= 1:
                        a.wait_ge(sem["f9"].h, 32 * e_)
                    a.activation(gbc[:, tg, :], pb[5][:, 0:512], AF.Copy,
                                 scale=1.0 / ALPHA).then_inc(sem["g8"].h, 1)
                for j2 in range(32):
                    j_ = 32 * e_ + j2
                    a.wait_ge(sem["f7"].h, 2 * j_ + 1)
                    if j_ >= 2:
                        a.wait_ge(sem["f2"].h, F2G + 2 * (j_ - 2) + 2)
                    a.activation(sa_b[j_ % 2][:], at_b[j_ % 2][:], AF.Silu,
                                 scale=ALPHA).then_inc(sem["f8"].h, 1)
            sem["f8"].c += 32 * E
            a.wait_ge(sem["g7"].h, G7P + 4)
            for tg in range(2):
                ins = a.copy(R2b[:, 512 * tg:512 * (tg + 1)], pb[0 + tg][:, 0:512])
            ins.then_inc(sem["as_"].h, 1)
            sem["as_"].c += 1

        @blk.vector
        def _(v):
            for e_ in range(E):
                for oc in range(16):
                    for tg in range(2):
                        j_ = 2 * (16 * e_ + oc) + tg
                        v.wait_ge(sem["f6"].h, j_ + 1)
                        if j_ >= 2:
                            v.wait_ge(sem["f8"].h, j_ - 1)
                            v.wait_ge(sem["f9"].h, j_ - 1)
                        col = b1a_s[:, 16 * e_ + oc:16 * e_ + oc + 1]
                        v.tensor_scalar(at_b[j_ % 2][:], pb[0 + j_ % 2][:, 0:512],
                                        col, 7.0, ALU.add, ALU.min)\
                            .then_inc(sem["f7"].h, 1)
                        colb = b1b1_s[:, 16 * e_ + oc:16 * e_ + oc + 1]
                        v.tensor_scalar(c1_b[j_ % 2][:], pb[2 + j_ % 2][:, 0:512],
                                        colb, -6.0, ALU.add, ALU.max)\
                            .then_inc(sem["f7"].h, 1)
                        v.wait_ge(sem["f7"].h, 2 * j_ + 2)
                        v.tensor_scalar(c1_b[j_ % 2][:], c1_b[j_ % 2][:],
                                        8.0, None, ALU.min)\
                            .then_inc(sem["f2"].h, 1)
                        v.wait_ge(sem["f2"].h, F2G + 2 * j_ + 1)
                        v.wait_ge(sem["f8"].h, j_ + 1)
                        if j_ >= 2:
                            v.wait_ge(sem["f9"].h, j_ - 1)
                        v.tensor_mul(m2_b[j_ % 2][:], c1_b[j_ % 2][:],
                                     sa_b[j_ % 2][:]).then_inc(sem["f2"].h, 1)
                if e_ == E - 1:
                    v.wait_ge(sem["as_"].h, ASP + 1)
                    for tg in range(2):
                        ins = v.tensor_mul(C2b[:, 512 * tg:512 * (tg + 1)],
                                           pb[2 + tg][:, 0:512],
                                           R2b[:, 512 * tg:512 * (tg + 1)])
                    ins.then_inc(sem["h6"].h, 1)
                for dc in range(16):
                    for tg in range(2):
                        y_j = 2 * (16 * e_ + dc) + tg
                        v.wait_ge(sem["f10"].h, y_j + 1)
                        v.tensor_add(delta[:, dc, 512 * tg:512 * (tg + 1)],
                                     delta[:, dc, 512 * tg:512 * (tg + 1)],
                                     pb[4 + y_j % 2][:, 0:512])\
                            .then_inc(sem["f11"].h, 1)
                    if e_ == E - 1:
                        sh0 = sem["h0"] if dc % 2 == 0 else sem["h0b"]
                        base0 = H0_PRE if dc % 2 == 0 else H0b_PRE
                        v.wait_ge(sh0.h, base0 + 16 * (dc // 2 + 1))
                        if dc == 0:
                            v.wait_ge(sem["as_"].h, ASP + 1)
                        vc = V.inc(v.tensor_mul(xo2[dc % 2][:], xo2[dc % 2][:],
                                                R2b[:]))
                        v.wait_ge(V.h, vc)
                        vc = V.inc(v.tensor_add(xo2[dc % 2][:], xo2[dc % 2][:],
                                                C2b[:]))
                        v.wait_ge(V.h, vc)
                        v.tensor_add(delta[:, dc, :], delta[:, dc, :],
                                     xo2[dc % 2][:]).then_inc(sem["h1"].h, 1)
            sem["f7"].c += 2 * 32 * E
            sem["f2"].c += 2 * 32 * E
            sem["f11"].c += 32 * E
            sem["h1"].c += 16
            sem["h6"].c += 1

        @blk.sync
        def _(s):
            for dc in range(16):
                if dc >= 2:
                    s.wait_ge(sem["h1"].h, H1_PRE + dc - 1)
                sh0 = sem["h0"] if dc % 2 == 0 else sem["h0b"]
                base0 = H0_PRE if dc % 2 == 0 else H0b_PRE
                s.wait_ge(sh0.h, base0 + 16 * (dc // 2))
                s.dma_start(out=xo2[dc % 2][:],
                            in_=x2_scr[128 * dc:128 * (dc + 1), :])                    .then_inc(sh0.h, 16)
            sem["h0"].c += 128
            sem["h0b"].c += 128

    if "Gt" in tap_t or "delta" in tap_t:
        with nc.Block() as blk:
            @blk.gpsimd
            def _(g):
                if "Gt" in tap_t:
                    g.wait_ge(sem["f4"].h, GT_DONE)
                    g.dma_start(out=tap_t["Gt"], in_=Gt_s[:]).then_inc(sem["g9"].h, 16)
                    sem["g9"].c += 16
                if "delta" in tap_t:
                    g.wait_ge(sem["f11"].h, sem["f11"].c)
                    g.dma_start(out=tap_t["delta"], in_=delta[:]).then_inc(sem["g9"].h, 16)
                    sem["g9"].c += 16

    if ph == "ffn":
        return nc, tap_t

    for cm in reversed(ffn_scope):
        cm.__exit__(None, None, None)

    fin_scope = []

    def sbx(name, shape, dt):
        cm = nc.sbuf_tensor(name, shape, dt)
        t_ = cm.__enter__()
        fin_scope.append(cm)
        return t_

    rowsC = sbx("rowsC", [1, 6, TOK], F32)
    R3b = sbx("R3b", [128, TOK], F32)
    C3b = sbx("C3b", [128, TOK], F32)
    outb = [sbx(f"outb_{p}", [128, TOK], F32) for p in range(2)]

    with nc.Block() as blk:
        @blk.scalar
        def _(a):
            H2_0 = sem["h2"].c
            H5s = sem["h5"].c
            for dc in range(16):
                if dc >= 1:
                    a.wait_ge(sem["h5"].h, H5s + dc)
                a.activation(sqc[:], delta[:, dc, :], AF.Square)                    .then_inc(sem["h2"].h, 1)
            sem["h2"].c = H2_0 + 16

        @blk.tensor
        def _(t):
            H5_0 = sem["h5"].c
            for dc in range(16):
                for tg in range(2):
                    t.matmul(pb[4 + tg][0:1, 0:512], c_o128f[:],
                             delta[:, dc, 512 * tg:512 * (tg + 1)],
                             start=(dc == 0), stop=(dc == 15))
                t.wait_ge(sem["h2"].h, sem["h2"].c - 16 + dc + 1)
                for tg in range(2):
                    mm = t.matmul(pb[6 + tg][0:1, 0:512], c_o128b[:],
                                  sqc[:, 512 * tg:512 * (tg + 1)],
                                  start=(dc == 0), stop=(dc == 15))
                mm.then_inc(sem["h5"].h, 1)
            sem["h5"].c = H5_0 + 16

    with nc.Block() as blk:
        @blk.vector
        def _(v):
            v.wait_ge(sem["h5"].h, sem["h5"].c)
            vc = V.c
            for tg in range(2):
                vc = V.inc(v.tensor_scalar(rowsC[0:1, 1, 512 * tg:512 * (tg + 1)],
                                           pb[4 + tg][0:1, 0:512], RD, None,
                                           ALU.mult))
            v.wait_ge(V.h, vc)
            vc = V.inc(v.tensor_scalar(rowsC[0:1, 2, :], rowsC[0:1, 1, :],
                                       -1.0, None, ALU.mult))
            vc = V.inc(v.tensor_mul(rowsC[0:1, 3, :], rowsC[0:1, 1, :],
                                    rowsC[0:1, 1, :]))
            v.wait_ge(V.h, vc)
            for tg in range(2):
                ins = v.scalar_tensor_tensor(rowsC[0:1, 4, 512 * tg:512 * (tg + 1)],
                                             pb[6 + tg][0:1, 0:512], RD,
                                             rowsC[0:1, 3, 512 * tg:512 * (tg + 1)],
                                             ALU.mult, ALU.subtract)
            ins.then_inc(sem["h6"].h, 1)
            sem["h6"].c += 1

        @blk.scalar
        def _(a):
            a.wait_ge(sem["h6"].h, sem["h6"].c)
            a.activation(rowsC[0:1, 4, :], rowsC[0:1, 4, :], AF.Sqrt,
                         bias=c_eps[0:1, 0:1]).then_inc(sem["as_"].h, 1)
            sem["as_"].c += 1

        @blk.vector
        def _(v):
            A_.wait(v)
            vc = V.inc(v.reciprocal(rowsC[0:1, 5, :], rowsC[0:1, 4, :]))
            v.wait_ge(V.h, vc)
            ins = v.tensor_mul(rowsC[0:1, 2, :], rowsC[0:1, 2, :], rowsC[0:1, 5, :])
            ins.then_inc(sem["h6"].h, 1)
            sem["h6"].c += 1

        @blk.tensor
        def _(t):
            t.wait_ge(sem["h6"].h, sem["h6"].c)
            for tg in range(2):
                t.matmul(pb[0 + tg][:, 0:512], c_or128f[:],
                         rowsC[0:1, 5, 512 * tg:512 * (tg + 1)],
                         start=True, stop=True).then_inc(sem["g7"].h, 1)
            for tg in range(2):
                t.matmul(pb[2 + tg][:, 0:512], c_or128f[:],
                         rowsC[0:1, 2, 512 * tg:512 * (tg + 1)],
                         start=True, stop=True).then_inc(sem["g7"].h, 1)
            sem["g7"].c += 4

        @blk.scalar
        def _(a):
            a.wait_ge(sem["g7"].h, sem["g7"].c)
            for tg in range(2):
                a.copy(R3b[:, 512 * tg:512 * (tg + 1)], pb[0 + tg][:, 0:512])
            for tg in range(2):
                ins = a.copy(C3b[:, 512 * tg:512 * (tg + 1)], pb[2 + tg][:, 0:512])
            ins.then_inc(sem["as_"].h, 1)
            sem["as_"].c += 1

    FB2 = sem["as_"].c
    with nc.Block() as blk:
        @blk.sync
        def _(s):
            H3_0 = sem["h3"].c
            H7_0 = sem["h7"].c
            s.wait_ge(sem["h1"].h, sem["h1"].c)
            for dc in range(0, 16, 2):
                if dc >= 2:
                    s.wait_ge(sem["h3"].h, H3_0 + dc - 1)
                s.wait_ge(sem["h7"].h, H7_0 + 16 * (dc // 2))
                s.dma_start(out=xo2[0][:],
                            in_=x2_scr[128 * dc:128 * (dc + 1), :])                    .then_inc(sem["h7"].h, 16)
            sem["h7"].c += 128
            sem["fot"].c += 128
            sem["fot1"].c += 128
            s.wait_ge(sem["fot"].h, sem["fot"].c)
            s.wait_ge(sem["fot1"].h, sem["fot1"].c)

        @blk.scalar
        def _(a):
            H3_0 = sem["h3"].c
            H7b_0 = sem["h7b"].c
            a.wait_ge(sem["h1"].h, sem["h1"].c)
            for dc in range(1, 16, 2):
                if dc >= 2:
                    a.wait_ge(sem["h3"].h, H3_0 + dc - 1)
                a.wait_ge(sem["h7b"].h, H7b_0 + 16 * (dc // 2))
                a.dma_start(out=xo2[1][:],
                            in_=x2_scr[128 * dc:128 * (dc + 1), :])                    .then_inc(sem["h7b"].h, 16)
            sem["h7b"].c += 128

        @blk.vector
        def _(v):
            PMF = sem["pm"].c
            for dc in range(16):
                sh7 = sem["h7"] if dc % 2 == 0 else sem["h7b"]
                v.wait_ge(sh7.h, (sh7.c - 128) + 16 * (dc // 2 + 1))
                v.wait_ge(sem["pm"].h, PMF + dc + 1)
                vc = V.inc(v.tensor_add(outb[dc % 2][:], outb[dc % 2][:], C3b[:]))
                v.wait_ge(V.h, vc)
                v.tensor_add(outb[dc % 2][:], outb[dc % 2][:], xo2[dc % 2][:])                    .then_inc(sem["h3"].h, 1)
            sem["h3"].c += 16

        @blk.gpsimd
        def _(g):
            PMF = sem["pm"].c
            H3_0g = sem["h3"].c - 16
            g.wait_ge(sem["as_"].h, FB2)
            for dc in range(16):
                if dc >= 2:
                    g.wait_ge(sem["h3"].h, H3_0g + dc - 1)
                    sfo = sem["fot"] if dc % 2 == 0 else sem["fot1"]
                    g.wait_ge(sfo.h, (sfo.c - 128) + 16 * (dc // 2))
                g.tensor_mul(outb[dc % 2][:], delta[:, dc, :], R3b[:])\
                    .then_inc(sem["pm"].h, 1)
                if dc >= 1:
                    dp = dc - 1
                    g.wait_ge(sem["h3"].h, H3_0g + dp + 1)
                    sfp = sem["fot"] if dp % 2 == 0 else sem["fot1"]
                    g.wait_ge(sfp.h, (sfp.c - 128) + 16 * (dp // 2))
                    g.dma_start(out=outT[128 * dp:128 * (dp + 1), :],
                                in_=outb[dp % 2][:]).then_inc(sfp.h, 16)
            g.wait_ge(sem["h3"].h, H3_0g + 16)
            g.wait_ge(sem["fot1"].h, (sem["fot1"].c - 128) + 16 * 7)
            g.dma_start(out=outT[128 * 15:128 * 16, :],
                        in_=outb[1][:]).then_inc(sem["fot1"].h, 16)
            sem["pm"].c = PMF + 16

    return nc, tap_t



def _prep(inputs):
    f4 = np.float32
    x = np.asarray(inputs["L0_attn_x"], f4)
    Wq = np.asarray(inputs["L0_attn_Wq"], f4)
    Wk = np.asarray(inputs["L0_attn_Wk"], f4)
    Wv = np.asarray(inputs["L0_attn_Wv"], f4)
    Wo = np.asarray(inputs["L0_attn_Wo"], f4)
    nw = np.asarray(inputs["L0_ffn_norm_w"], f4)
    gwi = np.asarray(inputs["L0_ffn_gate_w"], f4)
    gbi = np.asarray(inputs["L0_ffn_gate_b"], f4)
    w1 = np.asarray(inputs["L0_ffn_w1"], f4)
    b1 = np.asarray(inputs["L0_ffn_b1"], f4)
    w2i = np.asarray(inputs["L0_ffn_w2"], f4)
    b2i = np.asarray(inputs["L0_ffn_b2"], f4)

    xTf = np.ascontiguousarray(x.transpose(0, 2, 1))

    def wtile(W, rowsel=None):
        Wt = W if rowsel is None else W[rowsel]
        O, K = Wt.shape
        r = Wt.reshape(O // 128, 128, K // 128, 128)
        return np.ascontiguousarray(r.transpose(0, 3, 2, 1))

    wo_dev = wtile(Wo)
    w1a_dev = np.stack([wtile(w1[e, 0::2, :]) for e in range(E)])
    w1b_dev = np.stack([wtile(w1[e, 1::2, :]) for e in range(E)])
    w2_dev = np.stack([wtile(w2i[e]) for e in range(E)])
    b1a_dev = np.ascontiguousarray(
        b1[:, 0::2].reshape(E, 16, 128).transpose(2, 0, 1))
    b1b_dev = np.ascontiguousarray(
        b1[:, 1::2].reshape(E, 16, 128).transpose(2, 0, 1))
    gw_dev = np.ascontiguousarray(
        gwi.transpose(1, 0).reshape(NKC, 128, E))
    gb_dev = np.ascontiguousarray(gbi[None, :])
    nwc_dev = np.ascontiguousarray(nw.reshape(16, 128).T)
    nwr_dev = np.ascontiguousarray(nw[None, :])

    in_maps = []
    for c in range(NCOR):
        hsel = slice(2 * c * 128, (2 * c + 2) * 128)
        wqk_dev = np.stack([
            np.stack([wtile(Wq)[2 * c + hh] for hh in range(HPC)]),
            np.stack([wtile(Wk)[2 * c + hh] for hh in range(HPC)]),
        ])
        wv_full = np.ascontiguousarray(Wv.T.reshape(NKC, 128, D))
        wv_dev = np.ascontiguousarray(wv_full[:, :, hsel])
        bc_ = c // 4
        t0 = TOK * (c % 4)
        in_maps.append(dict(
            xT=xTf, xTo=np.ascontiguousarray(xTf[bc_][:, t0:t0 + TOK]),
            wqk=wqk_dev, wv=wv_dev, wo=wo_dev,
            w1a=w1a_dev, w1b=w1b_dev, w2=w2_dev,
            b1a=b1a_dev, b1b=b1b_dev, b2=b2i,
            gw=gw_dev, gb=gb_dev, nwc=nwc_dev, nwr=nwr_dev,
        ))
    return in_maps


def kernel(**inputs):
    in_maps = _prep(inputs)
    nc, _ = build_program()
    res = run_bass_kernel_spmd(nc, in_maps, core_ids=list(range(NCOR)))
    out = np.zeros((B, T, D), np.float32)
    for c in range(NCOR):
        bc_ = c // 4
        t0 = TOK * (c % 4)
        out[bc_, t0:t0 + TOK, :] = np.asarray(res.results[c]["outT"]).T
    return out


# version 13
# speedup vs baseline: 1.0292x; 1.0292x over previous
import numpy as np
import ml_dtypes
import concourse.bass as bass
import concourse.mybir as mybir
from concourse.bass_utils import run_bass_kernel_spmd
from concourse.aot_env import allow_ici_all_to_all

F32 = mybir.dt.float32
BF16 = mybir.dt.bfloat16
AF = mybir.ActivationFunctionType
ALU = mybir.AluOpType
AX = mybir.AxisListType

B, T, D = 2, 4096, 2048
NCOR = 8
HPC = 2
NINST = 4
E = 16
TOK = 1024
NSTRIP = 16
NKC = 16
EPS = 1e-5
ALPHA = 1.702
SCL = float(1.0 / np.sqrt(128.0))
NEGINF = -1.0e30
RD = 1.0 / 2048.0

_PHASES = "all"


class S:
    def __init__(self, nc, name):
        self.h = nc.semaphore(name).__enter__()
        self.c = 0

    def inc(self, inst, n=1):
        inst.then_inc(self.h, n)
        self.c += n
        return self.c

    def wait(self, eng, c=None):
        eng.wait_ge(self.h, self.c if c is None else c)


def _phase_ge(a, b):
    order = ["a1", "a2", "a3", "ffn", "all"]
    return order.index(a) >= order.index(b)


def build_program(taps=()):
    taps = set(taps)
    ph = _PHASES
    nc = bass.Bass("TRN2", target_bir_lowering=False, debug=False)

    xT = nc.dram_tensor("xT", [B, D, T], F32, kind="ExternalInput").ap()
    xTo = nc.dram_tensor("xTo", [D, TOK], F32, kind="ExternalInput").ap()
    wqk = nc.dram_tensor("wqk", [2, HPC, 128, NKC, 128], F32, kind="ExternalInput").ap()
    wv = nc.dram_tensor("wv", [NKC, 128, HPC * 128], F32, kind="ExternalInput").ap()
    wo = nc.dram_tensor("wo", [16, 128, NKC, 128], F32, kind="ExternalInput").ap()
    w1a = nc.dram_tensor("w1a", [E, 16, 128, NKC, 128], F32, kind="ExternalInput").ap()
    w1b = nc.dram_tensor("w1b", [E, 16, 128, NKC, 128], F32, kind="ExternalInput").ap()
    w2 = nc.dram_tensor("w2", [E, 16, 128, NKC, 128], F32, kind="ExternalInput").ap()
    b1a = nc.dram_tensor("b1a", [128, E, 16], F32, kind="ExternalInput").ap()
    b1b = nc.dram_tensor("b1b", [128, E, 16], F32, kind="ExternalInput").ap()
    b2 = nc.dram_tensor("b2", [E, D], F32, kind="ExternalInput").ap()
    gw = nc.dram_tensor("gw", [NKC, 128, E], F32, kind="ExternalInput").ap()
    gb = nc.dram_tensor("gb", [1, E], F32, kind="ExternalInput").ap()
    nwc = nc.dram_tensor("nwc", [128, 16], F32, kind="ExternalInput").ap()
    nwr = nc.dram_tensor("nwr", [1, D], F32, kind="ExternalInput").ap()
    outT = nc.dram_tensor("outT", [D, TOK], F32, kind="ExternalOutput").ap()

    def inl(name, arr):
        return nc.inline_tensor(arr, name).ap()

    i_o128b = inl("i_o128b", np.ones((128, 1), ml_dtypes.bfloat16))
    i_or128f = inl("i_or128f", np.ones((1, 128), np.float32))
    i_idb = inl("i_idb", np.eye(128, dtype=ml_dtypes.bfloat16))
    i_idf = inl("i_idf", np.eye(128, dtype=np.float32))
    msk_np = np.zeros((4, 128, 512), np.float32)
    pc_ = np.arange(128)[:, None]
    jr_ = np.arange(512)[None, :]
    for i in range(4):
        msk_np[i] = np.where(128 * i + pc_ <= jr_, 0.0, NEGINF)
    i_masks = [inl(f"i_mask{i}", msk_np[i].astype(ml_dtypes.bfloat16)) for i in range(4)]
    oh_np = np.zeros((128, 16), np.float32)
    for h in range(4):
        oh_np[:, 4 * h + h] = 1.0
    i_oh4 = inl("i_oh4", oh_np.astype(ml_dtypes.bfloat16))
    oh16_np = np.zeros((16, 16 * 128), np.float32)
    for e in range(16):
        oh16_np[e, e * 128:(e + 1) * 128] = 1.0
    i_oh16 = inl("i_oh16", oh16_np.astype(ml_dtypes.bfloat16))
    i_eps = inl("i_eps", np.full((1, 1), EPS, np.float32))
    i_sclrow = inl("i_sclrow", np.full((1, 128), SCL, np.float32))
    i_o128f = inl("i_o128f", np.ones((128, 1), np.float32))

    k_scr = nc.dram_tensor("k_scr", [NINST, 128, T], BF16, kind="Internal").ap()
    v_scr = nc.dram_tensor("v_scr", [NINST, T, 128], BF16, kind="Internal").ap()
    zin_a = nc.dram_tensor("zin_a", [NCOR * 128, TOK], BF16, kind="Internal")
    zin_b = nc.dram_tensor("zin_b", [NCOR * 128, TOK], BF16, kind="Internal")
    zout_a = nc.dram_tensor("zout_a", [NCOR * 128, TOK], BF16, kind="Internal")
    zout_b = nc.dram_tensor("zout_b", [NCOR * 128, TOK], BF16, kind="Internal")
    x2_scr = nc.dram_tensor("x2_scr", [D, TOK], F32, kind="Internal").ap()

    tap_t = {}

    def mktap(name, shape, dt=F32):
        if name in taps:
            tap_t[name] = nc.dram_tensor("tap_" + name, shape, dt,
                                         kind="ExternalOutput").ap()

    mktap("q_s", [128, NINST, T])
    mktap("kvk", [NINST, 128, T])
    mktap("kvv", [NINST, T, 128])
    mktap("negB", [128, 4])
    mktap("z", [NCOR * 256, TOK])
    mktap("x2", [D, TOK])
    mktap("tt", [128, NKC, TOK])
    mktap("Gt", [16, TOK])
    mktap("delta", [128, NKC, TOK])

    sem = {n: S(nc, "sm_" + n) for n in
           ["ld0", "xb", "xfree", "sq", "stat", "statf", "row", "row2",
            "bc", "bcs", "col", "cole", "proj", "evq", "kvw", "sqr",
            "nmm", "nred", "vs", "as_", "ps", "finB", "ksq", "qsq",
            "a2ld", "a2ldf", "sc", "ex", "av", "zdone", "rr", "bc6",
            "rls", "zev", "zw", "cc", "phb",
            "g0", "g1", "g2", "g3", "g4", "g5", "g6", "g7", "g8", "g9",
            "h0", "h1", "h2", "h3", "h4", "h5", "h6", "h7", "h8", "h9",
            "f0", "f1", "f2", "f3", "f4", "f5", "f6", "f7", "f8", "f9",
            "f10", "f11", "fot", "fot1", "ldg", "xb1", "kvw1", "a2l1",
            "zw1", "g0b", "h0b", "h4b", "h7b", "pm",
            "wa0", "wa1", "wb0", "wb1", "wc0", "wc1"]}
    V, A_, P_ = sem["vs"], sem["as_"], sem["ps"]

    ctxs = []

    def sb(name, shape, dt):
        cm = nc.sbuf_tensor(name, shape, dt)
        t_ = cm.__enter__()
        ctxs.append(cm)
        return t_

    c_o128b = sb("s_o128b", [128, 1], BF16)
    c_or128f = sb("s_or128f", [1, 128], F32)
    c_idb = sb("s_idb", [128, 128], BF16)
    c_idf = sb("s_idf", [128, 128], F32)
    c_oh4 = sb("s_oh4", [128, 16], BF16)
    c_oh16 = sb("s_oh16", [16, 16 * 128], BF16)
    c_eps = sb("s_eps", [1, 1], F32)
    c_sclrow = sb("s_sclrow", [1, 128], F32)
    c_o128f = sb("s_o128f", [128, 1], F32)
    negB_s = sb("negB_s", [128, 4], F32)
    kn_max = sb("kn_max", [4, 1], F32)
    qn_max = sb("qn_max", [4, 1], F32)
    nrm_t = sb("nrm_t", [4, 4], F32)
    bn_row = sb("bn_row", [1, 4], F32)
    junk1 = sb("junk1", [1, 8], F32)

    pb = []
    for i in range(8):
        cm = nc.psum_tensor(f"pbank{i}", [128, 512], F32)
        pb.append(cm.__enter__())
        ctxs.append(cm)

    cnt = {}

    qkv_scope = []

    def sbq(name, shape, dt):
        cm = nc.sbuf_tensor(name, shape, dt)
        t_ = cm.__enter__()
        qkv_scope.append(cm)
        return t_

    c_masks = sbq("s_masks", [128, 4, 512], BF16)
    wqk_s = sbq("wqk_s", [128, 2, HPC, NKC, 128], BF16)
    wv_s = sbq("wv_s", [128, NKC, HPC * 128], BF16)
    q_s = sbq("q_sb", [128, NINST, T], BF16)
    wsum_qk = sbq("wsum_qk", [1, 2, HPC, 128], F32)
    wsum_v = sbq("wsum_v", [1, HPC * 128], F32)

    a1_scope = []

    def sba(name, shape, dt):
        cm = nc.sbuf_tensor(name, shape, dt)
        t_ = cm.__enter__()
        a1_scope.append(cm)
        return t_

    xb = [sba(f"xb{p}", [128, NKC, 512], BF16) for p in range(2)]
    sq_s = sba("sq_s", [128, NKC, 512], BF16)
    kst = [sba(f"kst{p}", [128, HPC, 512], BF16) for p in range(2)]
    vst = [sba(f"vst{p}", [128, 4, HPC * 128], BF16) for p in range(2)]
    rstb = [sba(f"rstb{p}", [128, 512], F32) for p in range(2)]
    rstqb = [sba(f"rstqb{p}", [128, 512], F32) for p in range(2)]
    rstc = [sba(f"rstc{p}", [128, 4], F32) for p in range(2)]
    rows = [sba(f"rows{p}", [1, 6, 512], F32) for p in range(2)]
    ksq_s = sba("ksq_s", [128, HPC, 512], BF16)
    qsq_s = sba("qsq_s", [128, HPC, 512], BF16)

    with nc.Block() as blk:
        @blk.sync
        def _(s):
            for dst, src in [(c_o128b, i_o128b), (c_or128f, i_or128f),
                             (c_idb, i_idb), (c_idf, i_idf),
                             (c_oh4, i_oh4), (c_oh16, i_oh16),
                             (c_eps, i_eps), (c_sclrow, i_sclrow),
                             (c_o128f, i_o128f)]:
                sem["ld0"].inc(s.dma_start(out=dst[:], in_=src), 16)
            for i in range(4):
                sem["ld0"].inc(s.dma_start(out=c_masks[:, i, :], in_=i_masks[i]), 16)

        @blk.gpsimd
        def _(g):
            sem["ldg"].inc(g.dma_start(
                out=wqk_s[:], in_=wqk.rearrange("p h kp kc o -> kp p h kc o")), 16)
            sem["ldg"].inc(g.dma_start(
                out=wv_s[:], in_=wv.rearrange("kc kp o -> kp kc o")), 16)

        @blk.tensor
        def _(t):
            sem["ld0"].wait(t)
            sem["ldg"].wait(t)
            mm = None
            first = True
            for pj in range(2):
                for hh in range(HPC):
                    for kc in range(NKC):
                        g_ = pj * HPC + hh
                        mm = t.matmul(pb[0][0:1, g_ * 128:(g_ + 1) * 128],
                                      c_o128b[:], wqk_s[:, pj, hh, kc, :],
                                      start=first,
                                      stop=(pj == 1 and hh == HPC - 1 and kc == NKC - 1))
                        first = False
            sem["proj"].inc(mm)
            for kc in range(NKC):
                mm = t.matmul(pb[1][0:1, 0:HPC * 128], c_o128b[:], wv_s[:, kc, :],
                              start=(kc == 0), stop=(kc == NKC - 1))
            sem["proj"].inc(mm)

        @blk.vector
        def _(v):
            sem["proj"].wait(v)
            V.inc(v.tensor_copy(wsum_qk[:].rearrange("a p h o -> a (p h o)"),
                                pb[0][0:1, 0:512]))
            V.inc(v.tensor_copy(wsum_v[:], pb[1][0:1, 0:HPC * 128]))
            V.inc(v.memset(kn_max[:], 0.0))
            V.inc(v.memset(qn_max[:], 0.0))
            sem["statf"].inc(v.engine_nop())
            sem["statf"].inc(v.engine_nop())

    PROJ0 = 2
    STATF0 = 2

    with nc.Block() as blk:
        @blk.gpsimd
        def _(g):
            for i in range(NSTRIP):
                b, si = divmod(i, 8)
                par = i % 2
                if i >= 2:
                    g.wait_ge(sem["proj"].h, PROJ0 + 8 * (i - 1))
                    g.wait_ge(sem["sq"].h, 4 * (i - 1))
                src = xT[b][:, 512 * si:512 * (si + 1)].rearrange(
                    "(kc kp) t -> kp kc t", kp=128)
                sxb = sem["xb"] if par == 0 else sem["xb1"]
                g.wait_ge(sxb.h, 16 * (i // 2))
                g.dma_start(out=xb[par][:], in_=src).then_inc(sxb.h, 16)
            sem["xb"].c = 16 * (NSTRIP // 2)
            sem["xb1"].c = 16 * (NSTRIP // 2)

        @blk.tensor
        def _(t):
            for i in range(NSTRIP):
                b, si = divmod(i, 8)
                par = i % 2
                t.wait_ge((sem["xb"] if par == 0 else sem["xb1"]).h,
                          16 * (i // 2 + 1))
                t.wait_ge(sem["statf"].h, STATF0 + 2 * i)
                for kc in range(NKC):
                    mm = t.matmul(pb[0][0:1, 0:512], c_o128b[:], xb[par][:, kc, :],
                                  start=(kc == 0), stop=(kc == NKC - 1))
                mm.then_inc(sem["stat"].h, 1)

                groups = [("q", 0), ("q", 1), ("k", 0), ("k", 1),
                          ("v", 0), ("v", 1), ("v", 2), ("v", 3)]

                def emit16(gidx):
                    kind, idx = groups[gidx]
                    bank = pb[5 + gidx % 3]
                    if i > 0 or gidx >= 3:
                        if gidx >= 3:
                            pg = 8 * i + (gidx - 3) + 1
                        else:
                            pg = 8 * (i - 1) + (gidx + 5) + 1
                        t.wait_ge(sem["evq"].h, pg)
                    if kind == "q":
                        for kc in range(NKC):
                            t.matmul(bank[:, 0:512], wqk_s[:, 0, idx, kc, :],
                                     xb[par][:, kc, :], start=(kc == 0), stop=False)
                    elif kind == "k":
                        for kc in range(NKC):
                            t.matmul(bank[:, 0:512], wqk_s[:, 1, idx, kc, :],
                                     xb[par][:, kc, :], start=(kc == 0), stop=False)
                    else:
                        for kc in range(NKC):
                            t.matmul(bank[:, 0:HPC * 128],
                                     xb[par][:, kc, 128 * idx:128 * (idx + 1)],
                                     wv_s[:, kc, :], start=(kc == 0), stop=False)

                def emit_corr(gidx):
                    kind, idx = groups[gidx]
                    bank = pb[5 + gidx % 3]
                    nm = rows[par][0:1, 1, :]
                    if kind == "q":
                        mm = t.matmul(bank[:, 0:512], wsum_qk[0:1, 0, idx, :], nm,
                                      start=False, stop=True)
                    elif kind == "k":
                        mm = t.matmul(bank[:, 0:512], wsum_qk[0:1, 1, idx, :], nm,
                                      start=False, stop=True)
                    else:
                        mm = t.matmul(bank[:, 0:HPC * 128],
                                      nm[:, 128 * idx:128 * (idx + 1)], wsum_v[:],
                                      start=False, stop=True)
                    mm.then_inc(sem["proj"].h, 1)
                    return mm

                emit16(0)
                emit16(1)
                t.wait_ge(sem["statf"].h, STATF0 + 2 * i)
                t.wait_ge(sem["sq"].h, 4 * (i + 1))
                for kc in range(NKC):
                    mm = t.matmul(pb[1][0:1, 0:512], c_o128b[:], sq_s[:, kc, :],
                                  start=(kc == 0), stop=(kc == NKC - 1))
                mm.then_inc(sem["stat"].h, 1)
                emit16(2)
                t.wait_ge(sem["row2"].h, i + 1)
                if i > 0:
                    t.wait_ge(sem["bcs"].h, 2 * (i - 1) + 1)
                t.matmul(pb[2][:, 0:512], c_or128f[:], rows[par][0:1, 5, :],
                         start=True, stop=True).then_inc(sem["bc"].h, 1)
                if i > 0:
                    t.wait_ge(sem["bcs"].h, 2 * (i - 1) + 2)
                t.matmul(pb[3][:, 0:512], c_sclrow[:], rows[par][0:1, 5, :],
                         start=True, stop=True).then_inc(sem["bc"].h, 1)
                if i > 0:
                    t.wait_ge(sem["nred"].h, 2 * i)
                for st in range(4):
                    mm = t.matmul(pb[4][:, st:st + 1],
                                  rows[par][0:1, 5, 128 * st:128 * (st + 1)],
                                  c_or128f[0:1, 0:1], start=(st == 0), stop=(st == 3))
                mm.then_inc(sem["col"].h, 1)
                for gidx in range(3, 8):
                    emit_corr(gidx - 3)
                    emit16(gidx)
                for gidx in range(5, 8):
                    mm = emit_corr(gidx)
                t.wait_ge(sem["cole"].h, i + 1)
                t.wait_ge(sem["ksq"].h, i + 1)
                for hh in range(HPC):
                    mm = t.matmul(pb[4][0:4, 0:512],
                                  c_oh4[:, 4 * (2 * b + hh):4 * (2 * b + hh) + 4],
                                  ksq_s[:, hh, :], start=(hh == 0), stop=(hh == HPC - 1))
                mm.then_inc(sem["nmm"].h, 1)
                t.wait_ge(sem["nred"].h, 2 * i + 1)
                t.wait_ge(sem["qsq"].h, i + 1)
                for hh in range(HPC):
                    mm = t.matmul(pb[4][0:4, 0:512],
                                  c_oh4[:, 4 * (2 * b + hh):4 * (2 * b + hh) + 4],
                                  qsq_s[:, hh, :], start=(hh == 0), stop=(hh == HPC - 1))
                mm.then_inc(sem["nmm"].h, 1)
            sem["stat"].c = 2 * NSTRIP
            sem["bc"].c = 2 * NSTRIP
            sem["col"].c = NSTRIP
            sem["nmm"].c = 2 * NSTRIP
            sem["proj"].c = PROJ0 + 8 * NSTRIP

        @blk.scalar
        def _(a):
            for i in range(NSTRIP):
                par = i % 2
                a.wait_ge((sem["xb"] if par == 0 else sem["xb1"]).h,
                          16 * (i // 2 + 1))
                if i > 0:
                    a.wait_ge(sem["stat"].h, 2 * i)
                for j in range(4):
                    ins = a.activation(sq_s[:, 4 * j:4 * (j + 1), :],
                                       xb[par][:, 4 * j:4 * (j + 1), :], AF.Square)
                    ins.then_inc(sem["sq"].h, 1)
                a.wait_ge(sem["statf"].h, STATF0 + 2 * i + 2)
                a.activation(rows[par][0:1, 4, :], rows[par][0:1, 3, :],
                             AF.Sqrt, bias=c_eps[0:1, 0:1]).then_inc(sem["sqr"].h, 1)
                a.wait_ge(sem["bc"].h, 2 * i + 1)
                if i > 1:
                    a.wait_ge(sem["evq"].h, 8 * (i - 1))
                a.copy(rstb[par][:], pb[2][:, 0:512]).then_inc(sem["bcs"].h, 1)
                a.wait_ge(sem["bc"].h, 2 * i + 2)
                a.copy(rstqb[par][:], pb[3][:, 0:512]).then_inc(sem["bcs"].h, 1)
                a.wait_ge(sem["col"].h, i + 1)
                a.copy(rstc[par][:], pb[4][:, 0:4]).then_inc(sem["cole"].h, 1)
                if i > 0:
                    a.wait_ge(sem["nmm"].h, 2 * i)
                a.wait_ge(sem["evq"].h, 8 * i + 4)
                a.activation(ksq_s[:], kst[par][:], AF.Square).then_inc(sem["ksq"].h, 1)
                b_, si = divmod(i, 8)
                insts = [2 * b_ + 0, 2 * b_ + 1]
                for hh in range(HPC):
                    ins = a.activation(qsq_s[:, hh, :],
                                       q_s[:, insts[hh], 512 * si:512 * (si + 1)],
                                       AF.Square)
                ins.then_inc(sem["qsq"].h, 1)
            sem["sq"].c = 4 * NSTRIP
            sem["sqr"].c = NSTRIP
            sem["bcs"].c = 2 * NSTRIP
            sem["cole"].c = NSTRIP
            sem["ksq"].c = NSTRIP
            sem["qsq"].c = NSTRIP

        @blk.vector
        def _(v):
            vc = V.c
            for i in range(NSTRIP):
                b_, si = divmod(i, 8)
                par = i % 2
                if i > 1:
                    v.wait_ge((sem["kvw"] if par == 0 else sem["kvw1"]).h,
                              160 * (i // 2))
                v.wait_ge(sem["stat"].h, 2 * i + 1)
                if i > 1:
                    v.wait_ge(sem["col"].h, i - 1)
                vc = V.inc(v.tensor_scalar(rows[par][0:1, 0, :], pb[0][0:1, 0:512],
                                           RD, None, ALU.mult))
                v.wait_ge(V.h, vc)
                ins = v.tensor_scalar(rows[par][0:1, 1, :], rows[par][0:1, 0, :],
                                      -1.0, None, ALU.mult)
                ins.then_inc(sem["row"].h, 1)
                vc = V.inc(v.tensor_mul(rows[par][0:1, 2, :], rows[par][0:1, 0, :],
                                        rows[par][0:1, 0, :]))
                v.wait_ge(sem["stat"].h, 2 * i + 2)
                v.wait_ge(V.h, vc)
                ins = v.scalar_tensor_tensor(rows[par][0:1, 3, :], pb[1][0:1, 0:512],
                                             RD, rows[par][0:1, 2, :],
                                             ALU.mult, ALU.subtract)
                ins.then_inc(sem["statf"].h, 2)
                v.wait_ge(sem["sqr"].h, i + 1)
                ins = v.reciprocal(rows[par][0:1, 5, :], rows[par][0:1, 4, :])
                ins.then_inc(sem["row2"].h, 1)
                insts = [2 * b_ + 0, 2 * b_ + 1]
                for g_ in range(8):
                    v.wait_ge(sem["proj"].h, PROJ0 + 8 * i + g_ + 1)
                    bank = pb[5 + g_ % 3]
                    if g_ < 2:
                        v.wait_ge(sem["bcs"].h, 2 * i + 2)
                        ins = v.tensor_mul(q_s[:, insts[g_], 512 * si:512 * (si + 1)],
                                           bank[:, 0:512], rstqb[par][:])
                    elif g_ < 4:
                        ins = v.tensor_mul(kst[par][:, g_ - 2, :],
                                           bank[:, 0:512], rstb[par][:])
                    else:
                        st = g_ - 4
                        v.wait_ge(sem["cole"].h, i + 1)
                        ins = v.tensor_scalar(vst[par][:, st, :], bank[:, 0:HPC * 128],
                                              rstc[par][:, st:st + 1], None, ALU.mult)
                    ins.then_inc(sem["evq"].h, 1)
                v.wait_ge(sem["nmm"].h, 2 * i + 1)
                vc = V.inc(v.reduce_max(nrm_t[0:4, 0:1], pb[4][0:4, 0:512], AX.X))
                v.wait_ge(V.h, vc)
                if i > 0:
                    v.wait_ge(sem["nred"].h, 2 * i - 1)
                ins = v.tensor_max(kn_max[:], kn_max[:], nrm_t[0:4, 0:1])
                ins.then_inc(sem["nred"].h, 1)
                v.wait_ge(sem["nmm"].h, 2 * i + 2)
                vc = V.inc(v.reduce_max(nrm_t[0:4, 1:2], pb[4][0:4, 0:512], AX.X))
                v.wait_ge(V.h, vc)
                if i > 0:
                    v.wait_ge(sem["nred"].h, 2 * i)
                ins = v.tensor_max(qn_max[:], qn_max[:], nrm_t[0:4, 1:2])
                ins.then_inc(sem["nred"].h, 1)
            sem["row"].c = NSTRIP
            sem["row2"].c = NSTRIP
            sem["statf"].c = STATF0 + 2 * NSTRIP
            sem["evq"].c = 8 * NSTRIP
            sem["nred"].c = 2 * NSTRIP

        @blk.sync
        def _(s):
            for i in range(NSTRIP):
                b_, si = divmod(i, 8)
                par = i % 2
                skv = sem["kvw"] if par == 0 else sem["kvw1"]
                s.wait_ge(skv.h, 160 * (i // 2))
                s.wait_ge(sem["evq"].h, 8 * i + 8)
                for hh in range(HPC):
                    inst = 2 * b_ + hh
                    s.dma_start(out=k_scr[inst, :, 512 * si:512 * (si + 1)],
                                in_=kst[par][:, hh, :]).then_inc(skv.h, 16)
                for st in range(4):
                    for hh in range(HPC):
                        inst = 2 * b_ + hh
                        s.dma_start(
                            out=v_scr[inst, 512 * si + 128 * st:512 * si + 128 * (st + 1), :],
                            in_=vst[par][:, st, 128 * hh:128 * (hh + 1)],
                        ).then_inc(skv.h, 16)
            sem["kvw"].c = 160 * (NSTRIP // 2)
            sem["kvw1"].c = 160 * (NSTRIP // 2)

    with nc.Block() as blk:
        @blk.vector
        def _(v):
            v.wait_ge(sem["nred"].h, 2 * NSTRIP)
            V.inc(v.tensor_mul(nrm_t[0:4, 2:3], kn_max[:], qn_max[:]))

        @blk.scalar
        def _(a):
            V.wait(a)
            A_.inc(a.activation(nrm_t[0:4, 3:4], nrm_t[0:4, 2:3], AF.Sqrt))

        @blk.vector
        def _(v):
            A_.wait(v)
            V.inc(v.tensor_scalar(bn_row[0:1, 0:4].tensor[0:4, 0:1] if False else nrm_t[0:4, 2:3],
                                  nrm_t[0:4, 3:4], -1.0, None, ALU.mult))

        @blk.tensor
        def _(t):
            V.wait(t)
            t.transpose(pb[4][0:1, 0:4], nrm_t[0:4, 2:3], c_idf[0:4, 0:4])\
                .then_inc(sem["finB"].h, 1)

        @blk.scalar
        def _(a):
            a.wait_ge(sem["finB"].h, 1)
            A_.inc(a.copy(bn_row[:], pb[4][0:1, 0:4]))

        @blk.tensor
        def _(t):
            A_.wait(t)
            for ih in range(4):
                mm = t.matmul(pb[4][:, ih:ih + 1], c_or128f[:],
                              bn_row[0:1, ih:ih + 1], start=(ih == 0), stop=(ih == 3))
            mm.then_inc(sem["finB"].h, 1)

        @blk.scalar
        def _(a):
            a.wait_ge(sem["finB"].h, 2)
            a.copy(negB_s[:], pb[4][:, 0:4]).then_inc(sem["finB"].h, 1)
        sem["finB"].c = 3

    if "q_s" in tap_t or "kvk" in tap_t or "negB" in tap_t:
        with nc.Block() as blk:
            @blk.vector
            def _(v):
                if "q_s" in tap_t:
                    v.wait_ge(sem["evq"].h, 8 * NSTRIP)
                if "negB" in tap_t:
                    v.wait_ge(sem["finB"].h, 3)
                V.inc(v.engine_nop())

            @blk.gpsimd
            def _(s):
                V.wait(s)
                n = 0
                if "q_s" in tap_t:
                    qf = tap_t["q_s"]
                    s.dma_start(out=qf, in_=q_s[:]).then_inc(sem["g9"].h, 16)
                    n += 16
                if "negB" in tap_t:
                    s.dma_start(out=tap_t["negB"], in_=negB_s[:]).then_inc(sem["g9"].h, 16)
                    n += 16
                if "kvk" in tap_t:
                    s.wait_ge(sem["kvw"].h, 160 * NSTRIP)
                    s.dma_start(out=tap_t["kvk"], in_=k_scr).then_inc(sem["g9"].h, 16)
                    s.dma_start(out=tap_t["kvv"], in_=v_scr).then_inc(sem["g9"].h, 16)
                    n += 32
                s.wait_ge(sem["g9"].h, n)
                sem["g9"].c = n

    if ph == "a1":
        return nc, tap_t

    for cm in reversed(a1_scope):
        cm.__exit__(None, None, None)
    a2_scope = []

    def sb2(name, shape, dt):
        cm = nc.sbuf_tensor(name, shape, dt)
        t_ = cm.__enter__()
        a2_scope.append(cm)
        return t_

    kh = [sb2(f"kh{p}", [128, 32, 128], BF16) for p in range(2)]
    vh = [sb2(f"vh{p}", [128, 32, 128], BF16) for p in range(2)]
    pt_s = sb2("pt_s", [128, 32, 512], BF16)
    zst = [sb2(f"zst{p}", [128, 512], BF16) for p in range(2)]
    rl_s = [sb2(f"rl{p}", [128, 512], F32) for p in range(2)]
    rr_s = sb2("rr_s", [1, 2, 512], F32)

    NS = [4 * (g + 1) for g in range(8)]
    IHSEQ = [0, 2, 1, 3]
    SCnt = []
    tot = 0
    for n in range(32):
        SCnt.append(tot)
        tot += NS[n % 8]
    TOTST = tot

    with nc.Block() as blk:
        @blk.sync
        def _(s):
            s.wait_ge(sem["kvw"].h, 160 * (NSTRIP // 2))
            s.wait_ge(sem["kvw1"].h, 160 * (NSTRIP // 2))
            s.wait_ge(sem["proj"].h, PROJ0 + 8 * NSTRIP)
            s.wait_ge(sem["sq"].h, 4 * NSTRIP)
            s.wait_ge(sem["stat"].h, 2 * NSTRIP)
            s.wait_ge(sem["evq"].h, 8 * NSTRIP)
            s.wait_ge(sem["nred"].h, 2 * NSTRIP)
            s.wait_ge(sem["finB"].h, 3)
            for pos in range(NINST):
                inst = IHSEQ[pos]
                par = pos % 2
                sld = sem["a2ld"] if par == 0 else sem["a2l1"]
                s.wait_ge(sld.h, 144 * (pos // 2))
                if pos >= 2:
                    s.wait_ge(sem["a2ldf"].h, pos - 1)
                s.dma_start(out=kh[par][:].rearrange("p st d -> p (st d)"),
                            in_=k_scr[inst]).then_inc(sld.h, 16)
                for j in range(8):
                    src = v_scr[inst, 512 * j:512 * (j + 1), :].rearrange(
                        "(st sp) d -> sp st d", sp=128)
                    s.dma_start(out=vh[par][:, 4 * j:4 * (j + 1), :], in_=src)\
                        .then_inc(sld.h, 16)
            sem["a2ld"].c = 16 * 9 * (NINST // 2)
            sem["a2l1"].c = 16 * 9 * (NINST // 2)

        @blk.tensor
        def _(t):
            t.wait_ge(sem["evq"].h, 8 * NSTRIP)
            slot_av = [0] * 32
            SCB = [0, 1, 7]
            bank_ex = [0, 0, 0]
            scn = 0
            avn = 0
            pend_bc = None
            for n in range(32):
                pos, g = divmod(n, 8)
                inst = IHSEQ[pos]
                par = pos % 2
                ns = NS[g]
                if g == 0:
                    t.wait_ge((sem["a2ld"] if par == 0 else sem["a2l1"]).h,
                              16 * 9 * (pos // 2 + 1))
                if n >= 2:
                    t.wait_ge(sem["zev"].h, n - 1)
                    t.wait_ge(sem["rr"].h, n - 1)
                zb = pb[2 + n % 2]
                lb = pb[4 + n % 2]
                for s_ in range(ns):
                    scb = pb[SCB[s_ % 3]]
                    if bank_ex[s_ % 3]:
                        t.wait_ge(sem["ex"].h, bank_ex[s_ % 3])
                    mm = t.matmul(scb[:, 0:512], kh[par][:, s_, :],
                                  q_s[:, inst, 512 * g:512 * (g + 1)],
                                  start=True, stop=(s_ < 4 * g))
                    if s_ >= 4 * g:
                        mm = t.matmul(scb[:, 0:512], c_idb[:],
                                      c_masks[:, s_ - 4 * g, :], start=False, stop=True)
                    scn += 1
                    mm.then_inc(sem["sc"].h, 1)
                    bank_ex[s_ % 3] = scn
                    if s_ > 0:
                        sp = s_ - 1
                        t.wait_ge(sem["ex"].h, SCnt[n] + sp + 1)
                        t.matmul(zb[:, 0:512], vh[par][:, sp, :], pt_s[:, sp, :],
                                 start=(sp == 0), stop=False)
                        mm = t.matmul(lb[0:1, 0:512], c_o128b[:], pt_s[:, sp, :],
                                      start=(sp == 0), stop=False)
                        avn += 1
                        mm.then_inc(sem["av"].h, 1)
                        slot_av[sp] = avn
                    if pend_bc is not None and s_ == min(ns - 2, 9):
                        np_, = pend_bc
                        t.wait_ge(sem["rr"].h, np_ + 1)
                        if np_ >= 2:
                            t.wait_ge(sem["rls"].h, np_ - 1)
                        t.matmul(pb[6][:, 0:512], c_or128f[:],
                                 rr_s[0:1, np_ % 2, :], start=True, stop=True)\
                            .then_inc(sem["bc6"].h, 1)
                        pend_bc = None
                sp = ns - 1
                t.wait_ge(sem["ex"].h, SCnt[n] + sp + 1)
                mm = t.matmul(zb[:, 0:512], vh[par][:, sp, :], pt_s[:, sp, :],
                              start=(sp == 0), stop=True)
                if g == 7:
                    mm.then_inc(sem["a2ldf"].h, 1)
                mm2 = t.matmul(lb[0:1, 0:512], c_o128b[:], pt_s[:, sp, :],
                               start=(sp == 0), stop=True)
                avn += 1
                mm2.then_inc(sem["av"].h, 1)
                slot_av[sp] = avn
                pend_bc = (n,)
            np_, = pend_bc
            t.wait_ge(sem["rr"].h, np_ + 1)
            t.wait_ge(sem["rls"].h, np_ - 1)
            t.matmul(pb[6][:, 0:512], c_or128f[:], rr_s[0:1, np_ % 2, :],
                     start=True, stop=True).then_inc(sem["bc6"].h, 1)
            sem["sc"].c = TOTST
            sem["av"].c = TOTST
            sem["bc6"].c = 32
            sem["a2ldf"].c = NINST
            cnt["a2_slot_av"] = slot_av

        @blk.scalar
        def _(a):
            a.wait_ge(sem["finB"].h, 3)
            slot_av = [0] * 32
            scn = 0
            avn = 0
            for n in range(32):
                pos, g = divmod(n, 8)
                inst = IHSEQ[pos]
                ns = NS[g]
                for s_ in range(ns):
                    scb = pb[[0, 1, 7][s_ % 3]]
                    scn += 1
                    a.wait_ge(sem["sc"].h, scn)
                    if slot_av[s_]:
                        a.wait_ge(sem["av"].h, slot_av[s_])
                    a.activation(pt_s[:, s_, :], scb[:, 0:512], AF.Exp,
                                 bias=negB_s[:, inst:inst + 1]).then_inc(sem["ex"].h, 1)
                for s_ in range(ns):
                    avn += 1
                    slot_av[s_] = avn
                if n >= 1:
                    m_ = n - 1
                    a.wait_ge(sem["bc6"].h, m_ + 1)
                    if m_ >= 2:
                        a.wait_ge(sem["zev"].h, m_ - 1)
                    a.copy(rl_s[m_ % 2][:], pb[6][:, 0:512]).then_inc(sem["rls"].h, 1)
            a.wait_ge(sem["bc6"].h, 32)
            a.copy(rl_s[31 % 2][:], pb[6][:, 0:512]).then_inc(sem["rls"].h, 1)
            sem["ex"].c = TOTST
            sem["rls"].c = 32

        @blk.vector
        def _(v):
            for n in range(32):
                ih, g = divmod(n, 8)
                ns = NS[g]
                v.wait_ge(sem["av"].h, SCnt[n] + ns)
                ins = v.reciprocal(rr_s[0:1, n % 2, :], pb[4 + n % 2][0:1, 0:512])
                ins.then_inc(sem["rr"].h, 1)
                v.wait_ge(sem["rls"].h, n + 1)
                if n >= 2:
                    v.wait_ge((sem["zw"] if n % 2 == 0 else sem["zw1"]).h,
                              16 * (n // 2))
                ins = v.tensor_mul(zst[n % 2][:], pb[2 + n % 2][:, 0:512],
                                   rl_s[n % 2][:])
                ins.then_inc(sem["zev"].h, 1)
            sem["rr"].c = 32
            sem["zev"].c = 32

        @blk.gpsimd
        def _(g_):
            for n in range(32):
                pos, g = divmod(n, 8)
                inst = IHSEQ[pos]
                b_ = inst // 2
                hh = inst % 2
                peer = b_ * 4 + g // 2
                zin_h = zin_a if hh == 0 else zin_b
                g_.wait_ge(sem["zev"].h, n + 1)
                dst = zin_h.ap()[peer * 128:(peer + 1) * 128,
                                 (g % 2) * 512:(g % 2) * 512 + 512]
                szw = sem["zw"] if n % 2 == 0 else sem["zw1"]
                g_.wait_ge(szw.h, 16 * (n // 2))
                g_.dma_start(out=dst, in_=zst[n % 2][:]).then_inc(szw.h, 16)
                if n == 15:
                    g_.wait_ge(sem["zw"].h, 16 * 8)
                    g_.wait_ge(sem["zw1"].h, 16 * 8)
                    with allow_ici_all_to_all():
                        g_.collective_compute(
                            "AllToAll", ALU.bypass,
                            replica_groups=[list(range(NCOR))],
                            ins=[zin_a.ap().opt()], outs=[zout_a.ap().opt()],
                        ).then_inc(sem["cc"].h, 1)
            sem["zw"].c = 16 * 16
            sem["zw1"].c = 16 * 16
            g_.wait_ge(sem["zw"].h, 16 * 16)
            g_.wait_ge(sem["zw1"].h, 16 * 16)
            with allow_ici_all_to_all():
                g_.collective_compute(
                    "AllToAll", ALU.bypass,
                    replica_groups=[list(range(NCOR))],
                    ins=[zin_b.ap().opt()], outs=[zout_b.ap().opt()],
                ).then_inc(sem["cc"].h, 1)
            sem["cc"].c = 2

    if "z" in tap_t:
        with nc.Block() as blk:
            @blk.gpsimd
            def _(s):
                s.wait_ge(sem["cc"].h, 2)
                s.dma_start(out=tap_t["z"][0:1024, :], in_=zout_a.ap())\
                    .then_inc(sem["g8"].h, 16)
                s.wait_ge(sem["g8"].h, 16)
                s.dma_start(out=tap_t["z"][1024:2048, :], in_=zout_b.ap())\
                    .then_inc(sem["g8"].h, 16)
                s.wait_ge(sem["g8"].h, 32)
                sem["g8"].c = 32

    if ph == "a2":
        return nc, tap_t

    for cm in reversed(a2_scope):
        cm.__exit__(None, None, None)
    for cm in reversed(qkv_scope):
        cm.__exit__(None, None, None)

    post_scope = []

    def sbp(name, shape, dt):
        cm = nc.sbuf_tensor(name, shape, dt)
        t_ = cm.__enter__()
        post_scope.append(cm)
        return t_

    t_s = sbp("t_sb", [128, NKC, TOK], BF16)
    delta = sbp("delta", [128, NKC, TOK], F32)
    Gt_s = sbp("Gt_s", [16, TOK], BF16)
    negm2r = sbp("negm2r", [1, TOK], F32)
    r2_r = sbp("r2_r", [1, TOK], F32)

    a3o_scope = []

    def sbo(name, shape, dt):
        cm = nc.sbuf_tensor(name, shape, dt)
        t_ = cm.__enter__()
        a3o_scope.append(cm)
        return t_

    ao_s = sbo("ao_s", [128, 16, TOK], BF16)

    p1_scope = []

    def sb1(name, shape, dt):
        cm = nc.sbuf_tensor(name, shape, dt)
        t_ = cm.__enter__()
        p1_scope.append(cm)
        return t_

    zf_s = sb1("zf_s", [128, 16, TOK], BF16)
    wo_b = [sb1(f"wo_b{p}", [128, NKC, 128], BF16) for p in range(2)]
    aosq = sb1("aosq", [128, TOK], BF16)

    with nc.Block() as blk:
        @blk.sync
        def _(s):
            s.wait_ge(sem["zev"].h, 32)
            s.wait_ge(sem["cc"].h, 1)
            s.dma_start(out=zf_s[:, 0:16:2, :],
                        in_=zout_a.ap().rearrange("(dc kp) t -> kp dc t", kp=128))                .then_inc(sem["g1"].h, 16)
            s.wait_ge(sem["cc"].h, 2)
            s.wait_ge(sem["g1"].h, 16)
            s.dma_start(out=zf_s[:, 1:16:2, :],
                        in_=zout_b.ap().rearrange("(dc kp) t -> kp dc t", kp=128))                .then_inc(sem["g1"].h, 16)
            sem["g1"].c += 32

        @blk.gpsimd
        def _(g):
            g.wait_ge(sem["zev"].h, 32)
            for oc in range(16):
                if oc >= 2:
                    g.wait_ge(sem["g2"].h, 2 * (oc - 1))
                sg0 = sem["g0"] if oc % 2 == 0 else sem["g0b"]
                g.wait_ge(sg0.h, 16 * (oc // 2))
                g.dma_start(out=wo_b[oc % 2][:], in_=wo[oc]).then_inc(sg0.h, 16)
            sem["g0"].c += 16 * 8
            sem["g0b"].c += 16 * 8

        @blk.tensor
        def _(t):
            t.wait_ge(sem["g1"].h, sem["g1"].c - 16)

            def stats_mm(oc):
                t.wait_ge(sem["g3"].h, 2 * (oc + 1))
                for tg in range(2):
                    t.matmul(pb[4 + tg][0:1, 0:512], c_o128b[:],
                             ao_s[:, oc, 512 * tg:512 * (tg + 1)],
                             start=(oc == 0), stop=(oc == 15))
                t.wait_ge(sem["g4"].h, oc + 1)
                for tg in range(2):
                    mm = t.matmul(pb[6 + tg][0:1, 0:512], c_o128b[:],
                                  aosq[:, 512 * tg:512 * (tg + 1)],
                                  start=(oc == 0), stop=(oc == 15))
                mm.then_inc(sem["g5"].h, 1)

            evens = list(range(0, NKC, 2))
            odds = list(range(1, NKC, 2))
            for oc in range(2):
                t.wait_ge((sem["g0"] if oc % 2 == 0 else sem["g0b"]).h,
                          16 * (oc // 2 + 1))
                for tg in range(2):
                    bank = pb[2 * (oc % 2) + tg]
                    for ki, kc in enumerate(evens):
                        t.matmul(bank[:, 0:512], wo_b[oc % 2][:, kc, :],
                                 zf_s[:, kc, 512 * tg:512 * (tg + 1)],
                                 start=(ki == 0), stop=False)
            t.wait_ge(sem["g1"].h, sem["g1"].c)
            for oc in range(2):
                for tg in range(2):
                    bank = pb[2 * (oc % 2) + tg]
                    for ki, kc in enumerate(odds):
                        mm = t.matmul(bank[:, 0:512], wo_b[oc % 2][:, kc, :],
                                      zf_s[:, kc, 512 * tg:512 * (tg + 1)],
                                      start=False, stop=(ki == len(odds) - 1))
                    mm.then_inc(sem["g2"].h, 1)
            stats_mm(0)
            for oc in range(2, 16):
                t.wait_ge((sem["g0"] if oc % 2 == 0 else sem["g0b"]).h,
                          16 * (oc // 2 + 1))
                for tg in range(2):
                    bank = pb[2 * (oc % 2) + tg]
                    t.wait_ge(sem["g3"].h, 2 * (oc - 1))
                    for kc in range(NKC):
                        mm = t.matmul(bank[:, 0:512], wo_b[oc % 2][:, kc, :],
                                      zf_s[:, kc, 512 * tg:512 * (tg + 1)],
                                      start=(kc == 0), stop=(kc == NKC - 1))
                    mm.then_inc(sem["g2"].h, 1)
                stats_mm(oc - 1)
            stats_mm(15)
            sem["g2"].c += 32
            sem["g5"].c += 16

        @blk.scalar
        def _(a):
            for oc in range(16):
                for tg in range(2):
                    a.wait_ge(sem["g2"].h, 2 * oc + tg + 1)
                    a.copy(ao_s[:, oc, 512 * tg:512 * (tg + 1)],
                           pb[2 * (oc % 2) + tg][:, 0:512]).then_inc(sem["g3"].h, 1)
                if oc >= 1:
                    a.wait_ge(sem["g5"].h, oc)
                a.wait_ge(sem["g3"].h, 2 * (oc + 1))
                a.activation(aosq[:], ao_s[:, oc, :], AF.Square).then_inc(sem["g4"].h, 1)
            sem["g3"].c += 32
            sem["g4"].c += 16

    for cm in reversed(p1_scope):
        cm.__exit__(None, None, None)

    p2_scope = []

    def sb2b(name, shape, dt):
        cm = nc.sbuf_tensor(name, shape, dt)
        t_ = cm.__enter__()
        p2_scope.append(cm)
        return t_

    rowsB = sb2b("rowsB", [1, 5, TOK], F32)
    xo_b = [sb2b(f"xo_b{p}", [128, TOK], F32) for p in range(2)]
    xp_b = [sb2b(f"xp_b{p}", [128, TOK], F32) for p in range(2)]
    x2_b = [sb2b(f"x2_b{p}", [128, TOK], F32) for p in range(2)]
    tmpf = sb2b("tmpf", [128, TOK], F32)
    tmpg = sb2b("tmpg", [128, TOK], F32)
    sqb = sb2b("sqb", [128, TOK], BF16)
    Rb_s = sb2b("Rb_s", [128, TOK], F32)
    Cb_s = sb2b("Cb_s", [128, TOK], F32)
    nwc_s = sb2b("nwc_s", [128, 16], F32)

    with nc.Block() as blk:
        @blk.vector
        def _(v):
            v.wait_ge(sem["g5"].h, sem["g5"].c)
            vc = V.c
            for tg in range(2):
                vc = V.inc(v.tensor_scalar(rowsB[0:1, 0, 512 * tg:512 * (tg + 1)],
                                           pb[4 + tg][0:1, 0:512], RD, None, ALU.mult))
            v.wait_ge(V.h, vc)
            vc = V.inc(v.tensor_scalar(rowsB[0:1, 1, :], rowsB[0:1, 0, :],
                                       -1.0, None, ALU.mult))
            vc = V.inc(v.tensor_mul(rowsB[0:1, 2, :], rowsB[0:1, 0, :],
                                    rowsB[0:1, 0, :]))
            v.wait_ge(V.h, vc)
            for tg in range(2):
                ins = v.scalar_tensor_tensor(rowsB[0:1, 3, 512 * tg:512 * (tg + 1)],
                                             pb[6 + tg][0:1, 0:512], RD,
                                             rowsB[0:1, 2, 512 * tg:512 * (tg + 1)],
                                             ALU.mult, ALU.subtract)
            ins.then_inc(sem["g6"].h, 1)
            sem["g6"].c += 1

        @blk.scalar
        def _(a):
            a.wait_ge(sem["g6"].h, sem["g6"].c)
            a.activation(rowsB[0:1, 4, :], rowsB[0:1, 3, :], AF.Sqrt,
                         bias=c_eps[0:1, 0:1]).then_inc(sem["as_"].h, 1)
            sem["as_"].c += 1

        @blk.vector
        def _(v):
            A_.wait(v)
            vc = V.inc(v.reciprocal(rowsB[0:1, 2, :], rowsB[0:1, 4, :]))
            v.wait_ge(V.h, vc)
            ins = v.tensor_mul(rowsB[0:1, 3, :], rowsB[0:1, 1, :], rowsB[0:1, 2, :])
            ins.then_inc(sem["g6"].h, 1)
            sem["g6"].c += 1

    with nc.Block() as blk:
        @blk.tensor
        def _(t):
            t.wait_ge(sem["g6"].h, sem["g6"].c)
            t.wait_ge(sem["g3"].h, sem["g3"].c)
            for tg in range(2):
                t.matmul(pb[0 + tg][:, 0:512], c_or128f[:],
                         rowsB[0:1, 2, 512 * tg:512 * (tg + 1)],
                         start=True, stop=True).then_inc(sem["g7"].h, 1)
            for tg in range(2):
                t.matmul(pb[2 + tg][:, 0:512], c_or128f[:],
                         rowsB[0:1, 3, 512 * tg:512 * (tg + 1)],
                         start=True, stop=True).then_inc(sem["g7"].h, 1)
            sem["g7"].c += 4

        @blk.scalar
        def _(a):
            a.wait_ge(sem["g7"].h, sem["g7"].c)
            for tg in range(2):
                a.copy(Rb_s[:, 512 * tg:512 * (tg + 1)], pb[0 + tg][:, 0:512])
            for tg in range(2):
                ins = a.copy(Cb_s[:, 512 * tg:512 * (tg + 1)], pb[2 + tg][:, 0:512])
            ins.then_inc(sem["as_"].h, 1)
            sem["as_"].c += 1

        @blk.sync
        def _(s):
            s.wait_ge(sem["g5"].h, sem["g5"].c)
            s.wait_ge(sem["g1"].h, sem["g1"].c)
            s.dma_start(out=nwc_s[:], in_=nwc).then_inc(sem["g1"].h, 16)
            sem["g1"].c += 16

    A3A = sem["as_"].c
    with nc.Block() as blk:
        @blk.sync
        def _(s):
            for dc in range(0, 16, 2):
                if dc >= 2:
                    s.wait_ge(sem["h1"].h, dc - 1)
                s.wait_ge(sem["h0"].h, 16 * (dc // 2))
                s.dma_start(out=xo_b[0][:], in_=xTo[128 * dc:128 * (dc + 1), :])                    .then_inc(sem["h0"].h, 16)
            sem["h0"].c += 128
            sem["h0b"].c += 128

        @blk.gpsimd
        def _(g):
            PM1 = sem["pm"].c
            tpg = [tmpf, tmpg]
            for dc in range(16):
                if dc >= 2:
                    g.wait_ge(sem["h1"].h, dc - 1)
                g.tensor_mul(tpg[dc % 2][:], ao_s[:, dc, :], Rb_s[:])\
                    .then_inc(sem["pm"].h, 1)
                if dc >= 1:
                    g.wait_ge(sem["h1"].h, dc)
                    sh4 = sem["h4"] if (dc - 1) % 2 == 0 else sem["h4b"]
                    g.wait_ge(sh4.h, 16 * ((dc - 1) // 2))
                    g.dma_start(out=x2_scr[128 * (dc - 1):128 * dc, :],
                                in_=x2_b[(dc - 1) % 2][:]).then_inc(sh4.h, 16)
            g.wait_ge(sem["h1"].h, 16)
            g.dma_start(out=x2_scr[128 * 15:128 * 16, :],
                        in_=x2_b[1][:]).then_inc(sem["h4b"].h, 16)
            sem["pm"].c = PM1 + 16
            sem["h4"].c += 128
            sem["h4b"].c += 128

        @blk.vector
        def _(v):
            v.wait_ge(sem["as_"].h, A3A)
            PM1 = sem["pm"].c - 16
            tpg = [tmpf, tmpg]
            for dc in range(16):
                v.wait_ge((sem["h0"] if dc % 2 == 0 else sem["h0b"]).h,
                          16 * (dc // 2 + 1))
                if dc >= 2:
                    v.wait_ge(sem["h2"].h, dc - 1)
                    v.wait_ge(sem["h5"].h, dc - 1)
                    v.wait_ge((sem["h4"] if dc % 2 == 0 else sem["h4b"]).h,
                              16 * (dc // 2))
                vc = V.inc(v.tensor_add(x2_b[dc % 2][:], xo_b[dc % 2][:], Cb_s[:]))
                v.wait_ge(sem["pm"].h, PM1 + dc + 1)
                v.wait_ge(V.h, vc)
                ins = v.tensor_add(x2_b[dc % 2][:], x2_b[dc % 2][:],
                                   tpg[dc % 2][:])
                ins.then_inc(sem["h1"].h, 1)
            sem["h1"].c += 16

        @blk.scalar
        def _(a):
            for dc in range(16):
                if dc % 2 == 1:
                    if dc >= 2:
                        a.wait_ge(sem["h1"].h, dc - 1)
                    a.wait_ge(sem["h0b"].h, 16 * (dc // 2))
                    a.dma_start(out=xo_b[1][:],
                                in_=xTo[128 * dc:128 * (dc + 1), :])\
                        .then_inc(sem["h0b"].h, 16)
                a.wait_ge(sem["h1"].h, dc + 1)
                if dc >= 1:
                    a.wait_ge(sem["h5"].h, dc)
                a.activation(sqb[:], x2_b[dc % 2][:], AF.Square).then_inc(sem["h2"].h, 1)
            a.wait_ge(sem["h4b"].h, 16)
            a.dma_start(out=xp_b[1][:], in_=x2_scr[128:256, :])                .then_inc(sem["h7b"].h, 16)
            sem["h2"].c += 16

        @blk.sync
        def _(s):
            s.wait_ge(sem["h4"].h, 16)
            s.dma_start(out=xp_b[0][:], in_=x2_scr[0:128, :])                .then_inc(sem["h7"].h, 16)

        @blk.tensor
        def _(t):
            t.wait_ge(sem["g6"].h, sem["g6"].c)
            for dc in range(16):
                t.wait_ge(sem["h1"].h, dc + 1)
                for tg in range(2):
                    t.matmul(pb[4 + tg][0:1, 0:512], c_o128f[:],
                             x2_b[dc % 2][:, 512 * tg:512 * (tg + 1)],
                             start=(dc == 0), stop=(dc == 15))
                t.wait_ge(sem["h2"].h, dc + 1)
                for tg in range(2):
                    mm = t.matmul(pb[6 + tg][0:1, 0:512], c_o128b[:],
                                  sqb[:, 512 * tg:512 * (tg + 1)],
                                  start=(dc == 0), stop=(dc == 15))
                mm.then_inc(sem["h5"].h, 1)
            sem["h5"].c += 16

    with nc.Block() as blk:
        @blk.vector
        def _(v):
            v.wait_ge(sem["h5"].h, sem["h5"].c)
            vc = V.c
            for tg in range(2):
                vc = V.inc(v.tensor_scalar(rowsB[0:1, 0, 512 * tg:512 * (tg + 1)],
                                           pb[4 + tg][0:1, 0:512], RD, None, ALU.mult))
            v.wait_ge(V.h, vc)
            vc = V.inc(v.tensor_scalar(negm2r[:], rowsB[0:1, 0, :], -1.0,
                                       None, ALU.mult))
            vc = V.inc(v.tensor_mul(rowsB[0:1, 2, :], rowsB[0:1, 0, :],
                                    rowsB[0:1, 0, :]))
            v.wait_ge(V.h, vc)
            for tg in range(2):
                ins = v.scalar_tensor_tensor(rowsB[0:1, 3, 512 * tg:512 * (tg + 1)],
                                             pb[6 + tg][0:1, 0:512], RD,
                                             rowsB[0:1, 2, 512 * tg:512 * (tg + 1)],
                                             ALU.mult, ALU.subtract)
            ins.then_inc(sem["h6"].h, 1)
            sem["h6"].c += 1

        @blk.scalar
        def _(a):
            a.wait_ge(sem["h6"].h, sem["h6"].c)
            a.activation(rowsB[0:1, 4, :], rowsB[0:1, 3, :], AF.Sqrt,
                         bias=c_eps[0:1, 0:1]).then_inc(sem["as_"].h, 1)
            sem["as_"].c += 1

        @blk.vector
        def _(v):
            A_.wait(v)
            vc = V.inc(v.reciprocal(r2_r[:], rowsB[0:1, 4, :]))
            v.wait_ge(V.h, vc)
            vc = V.inc(v.tensor_mul(rowsB[0:1, 2, :], r2_r[:], r2_r[:]))
            v.wait_ge(V.h, vc)
            ins = v.tensor_mul(rowsB[0:1, 2, :], rowsB[0:1, 2, :], rowsB[0:1, 3, :])
            ins.then_inc(sem["h6"].h, 1)
            sem["h6"].c += 1

        @blk.scalar
        def _(a):
            a.wait_ge(sem["h6"].h, sem["h6"].c)
            a.activation(rowsB[0:1, 2, :], rowsB[0:1, 2, :], AF.Sqrt,
                         bias=c_eps[0:1, 0:1]).then_inc(sem["as_"].h, 1)
            sem["as_"].c += 1

        @blk.vector
        def _(v):
            A_.wait(v)
            vc = V.inc(v.reciprocal(rowsB[0:1, 2, :], rowsB[0:1, 2, :]))
            v.wait_ge(V.h, vc)
            vc = V.inc(v.tensor_mul(rowsB[0:1, 4, :], r2_r[:], rowsB[0:1, 2, :]))
            v.wait_ge(V.h, vc)
            ins = v.tensor_mul(rowsB[0:1, 3, :], negm2r[:], rowsB[0:1, 4, :])
            ins.then_inc(sem["h6"].h, 1)
            sem["h6"].c += 1

    PM3 = sem["pm"].c
    A3T = sem["as_"].c + 1
    with nc.Block() as blk:
        @blk.tensor
        def _(t):
            t.wait_ge(sem["h6"].h, sem["h6"].c)
            t.wait_ge(sem["h1"].h, sem["h1"].c)
            for tg in range(2):
                t.matmul(pb[0 + tg][:, 0:512], c_or128f[:],
                         rowsB[0:1, 4, 512 * tg:512 * (tg + 1)],
                         start=True, stop=True).then_inc(sem["g7"].h, 1)
            for tg in range(2):
                t.matmul(pb[2 + tg][:, 0:512], c_or128f[:],
                         rowsB[0:1, 3, 512 * tg:512 * (tg + 1)],
                         start=True, stop=True).then_inc(sem["g7"].h, 1)
            sem["g7"].c += 4

        @blk.scalar
        def _(a):
            a.wait_ge(sem["g7"].h, sem["g7"].c)
            for tg in range(2):
                a.copy(Rb_s[:, 512 * tg:512 * (tg + 1)], pb[0 + tg][:, 0:512])
            for tg in range(2):
                ins = a.copy(Cb_s[:, 512 * tg:512 * (tg + 1)], pb[2 + tg][:, 0:512])
            ins.then_inc(sem["as_"].h, 1)
            sem["as_"].c += 1
            for dc in range(3, 16, 2):
                a.wait_ge(sem["h4b"].h, 16 * (dc // 2 + 1))
                a.wait_ge(sem["pm"].h, PM3 + dc - 1)
                a.wait_ge(sem["h7b"].h, 16 * (dc // 2))
                a.dma_start(out=xp_b[1][:],
                            in_=x2_scr[128 * dc:128 * (dc + 1), :])                    .then_inc(sem["h7b"].h, 16)
            sem["h7b"].c += 128

        @blk.sync
        def _(s):
            for dc in range(2, 16, 2):
                s.wait_ge(sem["h4"].h, 16 * (dc // 2 + 1))
                s.wait_ge(sem["pm"].h, PM3 + dc - 1)
                s.wait_ge(sem["h7"].h, 16 * (dc // 2))
                s.dma_start(out=xp_b[0][:],
                            in_=x2_scr[128 * dc:128 * (dc + 1), :])                    .then_inc(sem["h7"].h, 16)
            sem["h7"].c += 128

        @blk.gpsimd
        def _(g):
            H9_0p = sem["h9"].c
            tpg = [tmpf, tmpg]
            g.wait_ge(sem["as_"].h, A3T)
            for dc in range(16):
                g.wait_ge((sem["h7"] if dc % 2 == 0 else sem["h7b"]).h,
                          16 * (dc // 2 + 1))
                if dc >= 2:
                    g.wait_ge(sem["h9"].h, H9_0p + 2 * (dc - 1))
                g.tensor_mul(tpg[dc % 2][:], xp_b[dc % 2][:], Rb_s[:])\
                    .then_inc(sem["pm"].h, 1)
            sem["pm"].c = PM3 + 16

        @blk.vector
        def _(v):
            v.wait_ge(sem["as_"].h, sem["as_"].c)
            v.wait_ge(sem["g1"].h, sem["g1"].c)
            H9_0 = sem["h9"].c
            tpg = [tmpf, tmpg]
            for dc in range(16):
                v.wait_ge(sem["pm"].h, PM3 + dc + 1)
                vc = V.inc(v.tensor_add(tpg[dc % 2][:], tpg[dc % 2][:], Cb_s[:]))
                v.wait_ge(V.h, vc)
                for tg in range(2):
                    ins = v.tensor_scalar(
                        t_s[:, dc, 512 * tg:512 * (tg + 1)],
                        tpg[dc % 2][:, 512 * tg:512 * (tg + 1)],
                        nwc_s[:, dc:dc + 1], None, ALU.mult)
                    ins.then_inc(sem["h9"].h, 1)
            sem["h9"].c += 32

    if "x2" in tap_t or "tt" in tap_t:
        with nc.Block() as blk:
            @blk.sync
            def _(s):
                if "x2" in tap_t:
                    s.wait_ge(sem["h4"].h, sem["h4"].c)
                    s.wait_ge(sem["h4b"].h, sem["h4b"].c)
                    s.dma_start(out=tap_t["x2"], in_=x2_scr).then_inc(sem["g9"].h, 16)
                    sem["g9"].c += 16

            @blk.gpsimd
            def _(g):
                if "tt" in tap_t:
                    g.wait_ge(sem["h9"].h, sem["h9"].c)
                    g.dma_start(out=tap_t["tt"], in_=t_s[:]).then_inc(sem["phb"].h, 16)
                    sem["phb"].c += 16

    if ph == "a3":
        return nc, tap_t

    for cm in reversed(p2_scope):
        cm.__exit__(None, None, None)
    for cm in reversed(a3o_scope):
        cm.__exit__(None, None, None)

    ffn_scope = []

    def sbf(name, shape, dt):
        cm = nc.sbuf_tensor(name, shape, dt)
        t_ = cm.__enter__()
        ffn_scope.append(cm)
        return t_

    finpre_scope = []

    def sbq(name, shape, dt):
        cm = nc.sbuf_tensor(name, shape, dt)
        t_ = cm.__enter__()
        finpre_scope.append(cm)
        return t_

    xo2 = [sbq(f"xo2_{p}", [128, TOK], F32) for p in range(2)]
    R2b = sbq("R2b", [128, TOK], F32)
    C2b = sbq("C2b", [128, TOK], F32)
    sqc = sbq("sqc", [128, TOK], BF16)

    act_s = sbf("act_s", [128, NKC, TOK], BF16)
    w1a_b = [sbf(f"w1a_b{p}", [128, NKC, 128], BF16) for p in range(2)]
    w1b_b = [sbf(f"w1b_b{p}", [128, NKC, 128], BF16) for p in range(2)]
    w2_b = [sbf(f"w2_b{p}", [128, NKC, 128], BF16) for p in range(2)]
    gbc = sbf("gbc", [128, 2, 512], BF16)
    b1a_s = sbf("b1a_s", [128, E * 16], F32)
    b1b_s = sbf("b1b_s", [128, E * 16], F32)
    b1b1_s = sbf("b1b1_s", [128, E * 16], F32)
    b2_s = sbf("b2_s", [16, D], BF16)
    gw_s = sbf("gw_s", [128, NKC, E], BF16)
    gb_s = sbf("gb_s", [1, E], F32)
    lg_s = sbf("lg_s", [128, 16], F32)
    junkg = sbf("junkg", [128, 512], F32)
    nmax_s = sbf("nmax_s", [128, 1], F32)
    eexp_s = sbf("eexp_s", [128, 16], F32)
    rkg_s = sbf("rkg_s", [128, 16], F32)
    rke_s = sbf("rke_s", [128, 16], F32)
    msk_s = sbf("msk_s", [128, 16], F32)
    den_s = sbf("den_s", [128, 2], F32)
    at_b = [sbf(f"at_b{p}", [128, 512], BF16) for p in range(2)]
    sa_b = [sbf(f"sa_b{p}", [128, 512], BF16) for p in range(2)]
    c1_b = [sbf(f"c1_b{p}", [128, 512], BF16) for p in range(2)]
    m2_b = [sbf(f"m2_b{p}", [128, 512], BF16) for p in range(2)]

    T_DONE = sem["h9"].c

    with nc.Block() as blk:
        @blk.sync
        def _(s):
            s.wait_ge(sem["h9"].h, T_DONE)
            s.dma_start(out=b1a_s[:], in_=b1a.rearrange("p e o -> p (e o)"))                .then_inc(sem["f1"].h, 16)
            s.dma_start(out=b1b_s[:], in_=b1b.rearrange("p e o -> p (e o)"))                .then_inc(sem["f1"].h, 16)
            s.dma_start(out=gb_s[:], in_=gb).then_inc(sem["f1"].h, 16)
            sem["f1"].c += 48

        @blk.gpsimd
        def _(g):
            g.wait_ge(sem["h9"].h, T_DONE)
            g.wait_ge(sem["ldg"].h, sem["ldg"].c)
            g.dma_start(out=b2_s[:], in_=b2).then_inc(sem["ldg"].h, 16)
            g.dma_start(out=gw_s[:], in_=gw.rearrange("kc kp e -> kp kc e"))                .then_inc(sem["ldg"].h, 16)
            sem["ldg"].c += 32

        @blk.vector
        def _(v):
            v.wait_ge(sem["f1"].h, sem["f1"].c)
            v.wait_ge(sem["ldg"].h, sem["ldg"].c)
            ins = v.tensor_scalar(b1b1_s[:], b1b_s[:], 1.0, None, ALU.add)
            ins.then_inc(sem["f3"].h, 1)
            sem["f3"].c += 1

    if ph == "smalls":
        with nc.Block() as blk:
            @blk.gpsimd
            def _(g):
                g.wait_ge(sem["f1"].h, sem["f1"].c)
                g.wait_ge(sem["f3"].h, sem["f3"].c)
                if "Gt" in tap_t:
                    g.dma_start(out=tap_t["Gt"], in_=Gt_s[:]).then_inc(sem["g9"].h, 16)
                    sem["g9"].c += 16
        return nc, tap_t

    F2_0 = sem["f2"].c
    F3_0 = sem["f3"].c
    F4_0 = sem["f4"].c
    A0 = sem["as_"].c
    V0 = sem["vs"].c
    with nc.Block() as blk:
        @blk.tensor
        def _(t):
            t.wait_ge(sem["f1"].h, sem["f1"].c)
            t.wait_ge(sem["ldg"].h, sem["ldg"].c)
            t.wait_ge(sem["h9"].h, T_DONE)
            t.wait_ge(sem["f3"].h, F3_0)

            def transpose_G(tt9):
                t.wait_ge(sem["f3"].h, F3_0 + 2 * tt9 + 2)
                if tt9 >= 1:
                    t.wait_ge(sem["f4"].h, F4_0 + 2 * tt9)
                t.transpose(pb[5][0:16, 0:128], lg_s[:], c_idf[:])\
                    .then_inc(sem["f4"].h, 1)

            for tt in range(8):
                bank = pb[6 + tt % 2]
                if tt >= 2:
                    t.wait_ge(sem["f3"].h, F3_0 + 2 * (tt - 2) + 1)
                for kc in range(NKC):
                    t.matmul(bank[:, 0:16], t_s[:, kc, 128 * tt:128 * (tt + 1)],
                             gw_s[:, kc, :], start=(kc == 0), stop=False)
                t.matmul(bank[:, 0:16], c_or128f[:], gb_s[:],
                         start=False, stop=True).then_inc(sem["f2"].h, 1)
                if tt >= 2:
                    transpose_G(tt - 2)
            transpose_G(6)
            transpose_G(7)
            sem["f2"].c += 8

        @blk.vector
        def _(v):
            for tt in range(8):
                v.wait_ge(sem["f2"].h, F2_0 + tt + 1)
                if tt >= 1:
                    v.wait_ge(sem["f4"].h, F4_0 + 2 * (tt - 1) + 1)
                v.tensor_copy(lg_s[:], pb[6 + tt % 2][:, 0:16])                    .then_inc(sem["f3"].h, 1)
                v.wait_ge(sem["f3"].h, F3_0 + 2 * tt + 1)
                V.inc(v.reduce_max(nmax_s[:], lg_s[:], AX.X, negate=True))
                for e_ in range(16):
                    v.tensor_scalar(junkg[:, 16 * e_:16 * (e_ + 1)], lg_s[:],
                                    lg_s[:, e_:e_ + 1], None,
                                    ALU.is_gt, ALU.add,
                                    accum_out=rkg_s[:, e_:e_ + 1])
                v.memset(rke_s[:, 0:1], 0.0)
                for e_ in range(1, 16):
                    v.tensor_scalar(junkg[:, 256 + 16 * e_:256 + 16 * e_ + e_],
                                    lg_s[:, 0:e_],
                                    lg_s[:, e_:e_ + 1], None,
                                    ALU.is_equal, ALU.add,
                                    accum_out=rke_s[:, e_:e_ + 1])
                V.inc(v.engine_nop())
                v.wait_ge(V.h, V0 + 8 * tt + 2)
                V.inc(v.tensor_add(rkg_s[:], rkg_s[:], rke_s[:]))
                v.wait_ge(V.h, V0 + 8 * tt + 3)
                V.inc(v.tensor_scalar(msk_s[:], rkg_s[:], 4.0, None, ALU.is_lt))
                v.wait_ge(sem["as_"].h, A0 + tt + 1)
                v.wait_ge(V.h, V0 + 8 * tt + 4)
                V.inc(v.tensor_mul(eexp_s[:], eexp_s[:], msk_s[:]))
                v.wait_ge(V.h, V0 + 8 * tt + 5)
                V.inc(v.tensor_scalar(msk_s[:], eexp_s[:], 1.0, None,
                                      ALU.mult, ALU.add,
                                      accum_out=den_s[:, 0:1]))
                v.wait_ge(V.h, V0 + 8 * tt + 6)
                V.inc(v.reciprocal(den_s[:, 1:2], den_s[:, 0:1]))
                v.wait_ge(V.h, V0 + 8 * tt + 7)
                V.inc(v.tensor_scalar(lg_s[:], eexp_s[:], den_s[:, 1:2], None,
                                      ALU.mult))
                v.engine_nop().then_inc(sem["f3"].h, 1)
            sem["f3"].c += 16
            sem["vs"].c = V0 + 64

        @blk.scalar
        def _(a):
            def evac_G(k9):
                a.wait_ge(sem["f4"].h, F4_0 + 2 * k9 + 1)
                a.copy(Gt_s[0:16, 128 * k9:128 * (k9 + 1)], pb[5][0:16, 0:128])                    .then_inc(sem["f4"].h, 1)

            for tt in range(8):
                if tt >= 1:
                    evac_G(tt - 1)
                a.wait_ge(sem["vs"].h, V0 + 8 * tt + 1)
                if tt >= 1:
                    a.wait_ge(sem["f3"].h, F3_0 + 2 * (tt - 1) + 2)
                a.activation(eexp_s[:], lg_s[:], AF.Exp, bias=nmax_s[:])                    .then_inc(sem["as_"].h, 1)
            evac_G(7)
            sem["as_"].c += 8
            sem["f4"].c = F4_0 + 16

    GT_DONE = sem["f4"].c

    F5_0 = sem["f5"].c
    ZD_0 = sem["zdone"].c
    with nc.Block() as blk:
        @blk.tensor
        def _(t):
            t.wait_ge(sem["f4"].h, GT_DONE)
            for i_ in range(32):
                dc, tg = divmod(i_, 2)
                if i_ >= 2:
                    t.wait_ge(sem["zdone"].h, ZD_0 + i_ - 1)
                t.matmul(pb[4 + i_ % 2][:, 0:512],
                         b2_s[0:16, 128 * dc:128 * (dc + 1)],
                         Gt_s[0:16, 512 * tg:512 * (tg + 1)],
                         start=True, stop=True).then_inc(sem["f5"].h, 1)
            sem["f5"].c = F5_0 + 32

        @blk.scalar
        def _(a):
            for i_ in range(32):
                dc, tg = divmod(i_, 2)
                a.wait_ge(sem["f5"].h, F5_0 + i_ + 1)
                a.copy(delta[:, dc, 512 * tg:512 * (tg + 1)],
                       pb[4 + i_ % 2][:, 0:512]).then_inc(sem["zdone"].h, 1)
            sem["zdone"].c = ZD_0 + 32
    DI_DONE = sem["zdone"].c

    if ph == "gate":
        if "Gt" in tap_t:
            with nc.Block() as blk:
                @blk.gpsimd
                def _(g):
                    g.wait_ge(sem["f4"].h, GT_DONE)
                    g.wait_ge(sem["zdone"].h, DI_DONE)
                    g.dma_start(out=tap_t["Gt"], in_=Gt_s[:]).then_inc(sem["g9"].h, 16)
                    sem["g9"].c += 16
        return nc, tap_t

    G8_0 = sem["g8"].c
    F2G = sem["f2"].c
    F7_END = sem["f7"].c + 2 * 32 * E
    H6_PRE = sem["h6"].c
    H1_PRE = sem["h1"].c
    H2_PRE = sem["h2"].c
    H5_PRE = sem["h5"].c
    H0_PRE = sem["h0"].c
    H0b_PRE = sem["h0b"].c
    ASP = sem["as_"].c
    G7P = sem["g7"].c
    f0_at = {}
    with nc.Block() as blk:
        @blk.gpsimd
        def _(p):
            p.wait_ge(sem["h9"].h, T_DONE)

            wslot = {"wa0": sem["wa0"], "wa1": sem["wa1"],
                     "wb0": sem["wb0"], "wb1": sem["wb1"],
                     "wc0": sem["wc0"], "wc1": sem["wc1"]}

            def load_w1(u):
                e9, oc9 = divmod(u, 16)
                for nm, wten, wbuf in (("a", w1a, w1a_b), ("b", w1b, w1b_b)):
                    sl = wslot["w" + nm + str(oc9 % 2)]
                    p.wait_ge(sl.h, sl.c)
                    if u >= 2:
                        p.wait_ge(sem["f6"].h, 2 * (u - 2) + 2)
                    p.dma_start(out=wbuf[oc9 % 2][:], in_=wten[e9, oc9])\
                        .then_inc(sl.h, 16)
                    sl.c += 16
                    f0_at[("w1" + nm, u)] = sl.c

            def load_w2(v9):
                e9, dc9 = divmod(v9, 16)
                sl = wslot["wc" + str(dc9 % 2)]
                p.wait_ge(sl.h, sl.c)
                if v9 >= 2:
                    p.wait_ge(sem["f10"].h, 2 * (v9 - 2) + 2)
                p.dma_start(out=w2_b[dc9 % 2][:], in_=w2[e9, dc9])\
                    .then_inc(sl.h, 16)
                sl.c += 16
                f0_at[("w2", v9)] = sl.c

            load_w1(0)
            load_w1(1)
            for e_ in range(E):
                for oc in range(16):
                    u = 16 * e_ + oc
                    if u + 2 < 16 * E:
                        load_w1(u + 2)
                    if oc < 2:
                        load_w2(16 * e_ + oc)
                    for tg in range(2):
                        j_ = 2 * u + tg
                        p.wait_ge(sem["f2"].h, F2G + 2 * j_ + 2)
                        p.wait_ge(sem["g8"].h, G8_0 + 4 * e_ + 2 * (tg + 1))
                        if e_ >= 1:
                            p.wait_ge(sem["f10"].h, 32 * e_)
                        p.tensor_mul(act_s[:, oc, 512 * tg:512 * (tg + 1)],
                                     m2_b[j_ % 2][:], gbc[:, tg, :])\
                            .then_inc(sem["f9"].h, 1)
                for dc in range(2, 16):
                    load_w2(16 * e_ + dc)
            sem["f9"].c += 32 * E

        @blk.tensor
        def _(t):
            t.wait_ge(sem["f4"].h, GT_DONE)
            t.wait_ge(sem["zdone"].h, DI_DONE)
            for e_ in range(E):
                if e_ >= 1:
                    t.wait_ge(sem["f11"].h, 32 * e_)
                for tg in range(2):
                    gidx = 2 * e_ + tg
                    if gidx >= 1:
                        t.wait_ge(sem["g8"].h, G8_0 + 2 * gidx)
                    t.matmul(pb[5][:, 0:512],
                             c_oh16[0:16, 128 * e_:128 * (e_ + 1)],
                             Gt_s[0:16, 512 * tg:512 * (tg + 1)],
                             start=True, stop=True).then_inc(sem["g8"].h, 1)
                for oc in range(16):
                    u = 16 * e_ + oc
                    t.wait_ge(sem["wa" + str(oc % 2)].h, f0_at[("w1a", u)])
                    t.wait_ge(sem["wb" + str(oc % 2)].h, f0_at[("w1b", u)])
                    for tg in range(2):
                        j_ = 2 * u + tg
                        hA = pb[0 + j_ % 2]
                        hB = pb[2 + j_ % 2]
                        if j_ >= 2:
                            t.wait_ge(sem["f7"].h, 2 * (j_ - 2) + 2)
                        for kc in range(NKC):
                            t.matmul(hA[:, 0:512], w1a_b[oc % 2][:, kc, :],
                                     t_s[:, kc, 512 * tg:512 * (tg + 1)],
                                     start=(kc == 0), stop=(kc == NKC - 1))
                        for kc in range(NKC):
                            mm = t.matmul(hB[:, 0:512], w1b_b[oc % 2][:, kc, :],
                                          t_s[:, kc, 512 * tg:512 * (tg + 1)],
                                          start=(kc == 0), stop=(kc == NKC - 1))
                        mm.then_inc(sem["f6"].h, 1)
                t.wait_ge(sem["f9"].h, 32 * (e_ + 1))
                if e_ == E - 1:
                    t.wait_ge(sem["f7"].h, F7_END)
                    for tg in range(2):
                        t.matmul(pb[0 + tg][:, 0:512], c_or128f[:],
                                 r2_r[0:1, 512 * tg:512 * (tg + 1)],
                                 start=True, stop=True).then_inc(sem["g7"].h, 1)
                    for tg in range(2):
                        t.matmul(pb[2 + tg][:, 0:512], c_or128f[:],
                                 negm2r[0:1, 512 * tg:512 * (tg + 1)],
                                 start=True, stop=True).then_inc(sem["g7"].h, 1)
                for dc in range(16):
                    v9 = 16 * e_ + dc
                    t.wait_ge(sem["wc" + str(dc % 2)].h, f0_at[("w2", v9)])
                    for tg in range(2):
                        y_j = 2 * v9 + tg
                        yb = pb[4 + y_j % 2]
                        if y_j >= 2:
                            t.wait_ge(sem["f11"].h, y_j - 1)
                        for ic in range(NKC):
                            mm = t.matmul(yb[:, 0:512], w2_b[dc % 2][:, ic, :],
                                          act_s[:, ic, 512 * tg:512 * (tg + 1)],
                                          start=(ic == 0), stop=(ic == NKC - 1))
                        mm.then_inc(sem["f10"].h, 1)
            sem["g8"].c = G8_0 + 2 * E
            sem["f6"].c += 2 * 16 * E
            sem["f10"].c += 2 * 16 * E
            sem["g7"].c += 4

        @blk.scalar
        def _(a):
            for e_ in range(E):
                for tg in range(2):
                    gidx = 2 * e_ + tg
                    a.wait_ge(sem["g8"].h, G8_0 + 2 * gidx + 1)
                    if e_ >= 1:
                        a.wait_ge(sem["f9"].h, 32 * e_)
                    a.activation(gbc[:, tg, :], pb[5][:, 0:512], AF.Copy,
                                 scale=1.0 / ALPHA).then_inc(sem["g8"].h, 1)
                for j2 in range(32):
                    j_ = 32 * e_ + j2
                    a.wait_ge(sem["f7"].h, 2 * j_ + 1)
                    if j_ >= 2:
                        a.wait_ge(sem["f2"].h, F2G + 2 * (j_ - 2) + 2)
                    a.activation(sa_b[j_ % 2][:], at_b[j_ % 2][:], AF.Silu,
                                 scale=ALPHA).then_inc(sem["f8"].h, 1)
            sem["f8"].c += 32 * E
            a.wait_ge(sem["g7"].h, G7P + 4)
            for tg in range(2):
                ins = a.copy(R2b[:, 512 * tg:512 * (tg + 1)], pb[0 + tg][:, 0:512])
            ins.then_inc(sem["as_"].h, 1)
            sem["as_"].c += 1

        @blk.vector
        def _(v):
            for e_ in range(E):
                for oc in range(16):
                    for tg in range(2):
                        j_ = 2 * (16 * e_ + oc) + tg
                        v.wait_ge(sem["f6"].h, j_ + 1)
                        if j_ >= 2:
                            v.wait_ge(sem["f8"].h, j_ - 1)
                            v.wait_ge(sem["f9"].h, j_ - 1)
                        col = b1a_s[:, 16 * e_ + oc:16 * e_ + oc + 1]
                        v.tensor_scalar(at_b[j_ % 2][:], pb[0 + j_ % 2][:, 0:512],
                                        col, 7.0, ALU.add, ALU.min)\
                            .then_inc(sem["f7"].h, 1)
                        colb = b1b1_s[:, 16 * e_ + oc:16 * e_ + oc + 1]
                        v.tensor_scalar(c1_b[j_ % 2][:], pb[2 + j_ % 2][:, 0:512],
                                        colb, -6.0, ALU.add, ALU.max)\
                            .then_inc(sem["f7"].h, 1)
                        v.wait_ge(sem["f7"].h, 2 * j_ + 2)
                        v.tensor_scalar(c1_b[j_ % 2][:], c1_b[j_ % 2][:],
                                        8.0, None, ALU.min)\
                            .then_inc(sem["f2"].h, 1)
                        v.wait_ge(sem["f2"].h, F2G + 2 * j_ + 1)
                        v.wait_ge(sem["f8"].h, j_ + 1)
                        if j_ >= 2:
                            v.wait_ge(sem["f9"].h, j_ - 1)
                        v.tensor_mul(m2_b[j_ % 2][:], c1_b[j_ % 2][:],
                                     sa_b[j_ % 2][:]).then_inc(sem["f2"].h, 1)
                if e_ == E - 1:
                    v.wait_ge(sem["as_"].h, ASP + 1)
                    for tg in range(2):
                        ins = v.tensor_mul(C2b[:, 512 * tg:512 * (tg + 1)],
                                           pb[2 + tg][:, 0:512],
                                           R2b[:, 512 * tg:512 * (tg + 1)])
                    ins.then_inc(sem["h6"].h, 1)
                for dc in range(16):
                    for tg in range(2):
                        y_j = 2 * (16 * e_ + dc) + tg
                        v.wait_ge(sem["f10"].h, y_j + 1)
                        v.tensor_add(delta[:, dc, 512 * tg:512 * (tg + 1)],
                                     delta[:, dc, 512 * tg:512 * (tg + 1)],
                                     pb[4 + y_j % 2][:, 0:512])\
                            .then_inc(sem["f11"].h, 1)
                    if e_ == E - 1:
                        sh0 = sem["h0"] if dc % 2 == 0 else sem["h0b"]
                        base0 = H0_PRE if dc % 2 == 0 else H0b_PRE
                        v.wait_ge(sh0.h, base0 + 16 * (dc // 2 + 1))
                        if dc == 0:
                            v.wait_ge(sem["as_"].h, ASP + 1)
                        vc = V.inc(v.tensor_mul(xo2[dc % 2][:], xo2[dc % 2][:],
                                                R2b[:]))
                        v.wait_ge(V.h, vc)
                        vc = V.inc(v.tensor_add(xo2[dc % 2][:], xo2[dc % 2][:],
                                                C2b[:]))
                        v.wait_ge(V.h, vc)
                        v.tensor_add(delta[:, dc, :], delta[:, dc, :],
                                     xo2[dc % 2][:]).then_inc(sem["h1"].h, 1)
            sem["f7"].c += 2 * 32 * E
            sem["f2"].c += 2 * 32 * E
            sem["f11"].c += 32 * E
            sem["h1"].c += 16
            sem["h6"].c += 1

        @blk.sync
        def _(s):
            for dc in range(16):
                if dc >= 2:
                    s.wait_ge(sem["h1"].h, H1_PRE + dc - 1)
                sh0 = sem["h0"] if dc % 2 == 0 else sem["h0b"]
                base0 = H0_PRE if dc % 2 == 0 else H0b_PRE
                s.wait_ge(sh0.h, base0 + 16 * (dc // 2))
                s.dma_start(out=xo2[dc % 2][:],
                            in_=x2_scr[128 * dc:128 * (dc + 1), :])                    .then_inc(sh0.h, 16)
            sem["h0"].c += 128
            sem["h0b"].c += 128

    if "Gt" in tap_t or "delta" in tap_t:
        with nc.Block() as blk:
            @blk.gpsimd
            def _(g):
                if "Gt" in tap_t:
                    g.wait_ge(sem["f4"].h, GT_DONE)
                    g.dma_start(out=tap_t["Gt"], in_=Gt_s[:]).then_inc(sem["g9"].h, 16)
                    sem["g9"].c += 16
                if "delta" in tap_t:
                    g.wait_ge(sem["f11"].h, sem["f11"].c)
                    g.dma_start(out=tap_t["delta"], in_=delta[:]).then_inc(sem["g9"].h, 16)
                    sem["g9"].c += 16

    if ph == "ffn":
        return nc, tap_t

    for cm in reversed(ffn_scope):
        cm.__exit__(None, None, None)

    fin_scope = []

    def sbx(name, shape, dt):
        cm = nc.sbuf_tensor(name, shape, dt)
        t_ = cm.__enter__()
        fin_scope.append(cm)
        return t_

    rowsC = sbx("rowsC", [1, 6, TOK], F32)
    R3b = sbx("R3b", [128, TOK], F32)
    C3b = sbx("C3b", [128, TOK], F32)
    outb = [sbx(f"outb_{p}", [128, TOK], F32) for p in range(2)]

    with nc.Block() as blk:
        @blk.scalar
        def _(a):
            H2_0 = sem["h2"].c
            H5s = sem["h5"].c
            for dc in range(16):
                if dc >= 1:
                    a.wait_ge(sem["h5"].h, H5s + dc)
                a.activation(sqc[:], delta[:, dc, :], AF.Square)                    .then_inc(sem["h2"].h, 1)
            sem["h2"].c = H2_0 + 16

        @blk.tensor
        def _(t):
            H5_0 = sem["h5"].c
            for dc in range(16):
                for tg in range(2):
                    t.matmul(pb[4 + tg][0:1, 0:512], c_o128f[:],
                             delta[:, dc, 512 * tg:512 * (tg + 1)],
                             start=(dc == 0), stop=(dc == 15))
                t.wait_ge(sem["h2"].h, sem["h2"].c - 16 + dc + 1)
                for tg in range(2):
                    mm = t.matmul(pb[6 + tg][0:1, 0:512], c_o128b[:],
                                  sqc[:, 512 * tg:512 * (tg + 1)],
                                  start=(dc == 0), stop=(dc == 15))
                mm.then_inc(sem["h5"].h, 1)
            sem["h5"].c = H5_0 + 16

    with nc.Block() as blk:
        @blk.vector
        def _(v):
            v.wait_ge(sem["h5"].h, sem["h5"].c)
            vc = V.c
            for tg in range(2):
                vc = V.inc(v.tensor_scalar(rowsC[0:1, 1, 512 * tg:512 * (tg + 1)],
                                           pb[4 + tg][0:1, 0:512], RD, None,
                                           ALU.mult))
            v.wait_ge(V.h, vc)
            vc = V.inc(v.tensor_scalar(rowsC[0:1, 2, :], rowsC[0:1, 1, :],
                                       -1.0, None, ALU.mult))
            vc = V.inc(v.tensor_mul(rowsC[0:1, 3, :], rowsC[0:1, 1, :],
                                    rowsC[0:1, 1, :]))
            v.wait_ge(V.h, vc)
            for tg in range(2):
                ins = v.scalar_tensor_tensor(rowsC[0:1, 4, 512 * tg:512 * (tg + 1)],
                                             pb[6 + tg][0:1, 0:512], RD,
                                             rowsC[0:1, 3, 512 * tg:512 * (tg + 1)],
                                             ALU.mult, ALU.subtract)
            ins.then_inc(sem["h6"].h, 1)
            sem["h6"].c += 1

        @blk.scalar
        def _(a):
            a.wait_ge(sem["h6"].h, sem["h6"].c)
            a.activation(rowsC[0:1, 4, :], rowsC[0:1, 4, :], AF.Sqrt,
                         bias=c_eps[0:1, 0:1]).then_inc(sem["as_"].h, 1)
            sem["as_"].c += 1

        @blk.vector
        def _(v):
            A_.wait(v)
            vc = V.inc(v.reciprocal(rowsC[0:1, 5, :], rowsC[0:1, 4, :]))
            v.wait_ge(V.h, vc)
            ins = v.tensor_mul(rowsC[0:1, 2, :], rowsC[0:1, 2, :], rowsC[0:1, 5, :])
            ins.then_inc(sem["h6"].h, 1)
            sem["h6"].c += 1

        @blk.tensor
        def _(t):
            t.wait_ge(sem["h6"].h, sem["h6"].c)
            for tg in range(2):
                t.matmul(pb[0 + tg][:, 0:512], c_or128f[:],
                         rowsC[0:1, 5, 512 * tg:512 * (tg + 1)],
                         start=True, stop=True).then_inc(sem["g7"].h, 1)
            for tg in range(2):
                t.matmul(pb[2 + tg][:, 0:512], c_or128f[:],
                         rowsC[0:1, 2, 512 * tg:512 * (tg + 1)],
                         start=True, stop=True).then_inc(sem["g7"].h, 1)
            sem["g7"].c += 4

        @blk.scalar
        def _(a):
            a.wait_ge(sem["g7"].h, sem["g7"].c)
            for tg in range(2):
                a.copy(R3b[:, 512 * tg:512 * (tg + 1)], pb[0 + tg][:, 0:512])
            for tg in range(2):
                ins = a.copy(C3b[:, 512 * tg:512 * (tg + 1)], pb[2 + tg][:, 0:512])
            ins.then_inc(sem["as_"].h, 1)
            sem["as_"].c += 1

    FB2 = sem["as_"].c
    with nc.Block() as blk:
        @blk.sync
        def _(s):
            H3_0 = sem["h3"].c
            H7_0 = sem["h7"].c
            s.wait_ge(sem["h1"].h, sem["h1"].c)
            for dc in range(0, 16, 2):
                if dc >= 2:
                    s.wait_ge(sem["h3"].h, H3_0 + dc - 1)
                s.wait_ge(sem["h7"].h, H7_0 + 16 * (dc // 2))
                s.dma_start(out=xo2[0][:],
                            in_=x2_scr[128 * dc:128 * (dc + 1), :])                    .then_inc(sem["h7"].h, 16)
            sem["h7"].c += 128
            sem["fot"].c += 128
            sem["fot1"].c += 128
            s.wait_ge(sem["fot"].h, sem["fot"].c)
            s.wait_ge(sem["fot1"].h, sem["fot1"].c)

        @blk.scalar
        def _(a):
            H3_0 = sem["h3"].c
            H7b_0 = sem["h7b"].c
            a.wait_ge(sem["h1"].h, sem["h1"].c)
            for dc in range(1, 16, 2):
                if dc >= 2:
                    a.wait_ge(sem["h3"].h, H3_0 + dc - 1)
                a.wait_ge(sem["h7b"].h, H7b_0 + 16 * (dc // 2))
                a.dma_start(out=xo2[1][:],
                            in_=x2_scr[128 * dc:128 * (dc + 1), :])                    .then_inc(sem["h7b"].h, 16)
            sem["h7b"].c += 128

        @blk.vector
        def _(v):
            PMF = sem["pm"].c
            for dc in range(16):
                sh7 = sem["h7"] if dc % 2 == 0 else sem["h7b"]
                v.wait_ge(sh7.h, (sh7.c - 128) + 16 * (dc // 2 + 1))
                v.wait_ge(sem["pm"].h, PMF + dc + 1)
                vc = V.inc(v.tensor_add(outb[dc % 2][:], outb[dc % 2][:], C3b[:]))
                v.wait_ge(V.h, vc)
                v.tensor_add(outb[dc % 2][:], outb[dc % 2][:], xo2[dc % 2][:])                    .then_inc(sem["h3"].h, 1)
            sem["h3"].c += 16

        @blk.gpsimd
        def _(g):
            PMF = sem["pm"].c
            H3_0g = sem["h3"].c - 16
            g.wait_ge(sem["as_"].h, FB2)
            for dc in range(16):
                if dc >= 2:
                    g.wait_ge(sem["h3"].h, H3_0g + dc - 1)
                    sfo = sem["fot"] if dc % 2 == 0 else sem["fot1"]
                    g.wait_ge(sfo.h, (sfo.c - 128) + 16 * (dc // 2))
                g.tensor_mul(outb[dc % 2][:], delta[:, dc, :], R3b[:])\
                    .then_inc(sem["pm"].h, 1)
                if dc >= 1:
                    dp = dc - 1
                    g.wait_ge(sem["h3"].h, H3_0g + dp + 1)
                    sfp = sem["fot"] if dp % 2 == 0 else sem["fot1"]
                    g.wait_ge(sfp.h, (sfp.c - 128) + 16 * (dp // 2))
                    g.dma_start(out=outT[128 * dp:128 * (dp + 1), :],
                                in_=outb[dp % 2][:]).then_inc(sfp.h, 16)
            g.wait_ge(sem["h3"].h, H3_0g + 16)
            g.wait_ge(sem["fot1"].h, (sem["fot1"].c - 128) + 16 * 7)
            g.dma_start(out=outT[128 * 15:128 * 16, :],
                        in_=outb[1][:]).then_inc(sem["fot1"].h, 16)
            sem["pm"].c = PMF + 16

    return nc, tap_t



def _prep(inputs):
    f4 = np.float32
    x = np.asarray(inputs["L0_attn_x"], f4)
    Wq = np.asarray(inputs["L0_attn_Wq"], f4)
    Wk = np.asarray(inputs["L0_attn_Wk"], f4)
    Wv = np.asarray(inputs["L0_attn_Wv"], f4)
    Wo = np.asarray(inputs["L0_attn_Wo"], f4)
    nw = np.asarray(inputs["L0_ffn_norm_w"], f4)
    gwi = np.asarray(inputs["L0_ffn_gate_w"], f4)
    gbi = np.asarray(inputs["L0_ffn_gate_b"], f4)
    w1 = np.asarray(inputs["L0_ffn_w1"], f4)
    b1 = np.asarray(inputs["L0_ffn_b1"], f4)
    w2i = np.asarray(inputs["L0_ffn_w2"], f4)
    b2i = np.asarray(inputs["L0_ffn_b2"], f4)

    xTf = np.ascontiguousarray(x.transpose(0, 2, 1))

    def wtile(W, rowsel=None):
        Wt = W if rowsel is None else W[rowsel]
        O, K = Wt.shape
        r = Wt.reshape(O // 128, 128, K // 128, 128)
        return np.ascontiguousarray(r.transpose(0, 3, 2, 1))

    wo_dev = wtile(Wo)
    w1a_dev = np.stack([wtile(w1[e, 0::2, :]) for e in range(E)])
    w1b_dev = np.stack([wtile(w1[e, 1::2, :]) for e in range(E)])
    w2_dev = np.stack([wtile(w2i[e]) for e in range(E)])
    b1a_dev = np.ascontiguousarray(
        b1[:, 0::2].reshape(E, 16, 128).transpose(2, 0, 1))
    b1b_dev = np.ascontiguousarray(
        b1[:, 1::2].reshape(E, 16, 128).transpose(2, 0, 1))
    gw_dev = np.ascontiguousarray(
        gwi.transpose(1, 0).reshape(NKC, 128, E))
    gb_dev = np.ascontiguousarray(gbi[None, :])
    nwc_dev = np.ascontiguousarray(nw.reshape(16, 128).T)
    nwr_dev = np.ascontiguousarray(nw[None, :])

    in_maps = []
    for c in range(NCOR):
        hsel = slice(2 * c * 128, (2 * c + 2) * 128)
        wqk_dev = np.stack([
            np.stack([wtile(Wq)[2 * c + hh] for hh in range(HPC)]),
            np.stack([wtile(Wk)[2 * c + hh] for hh in range(HPC)]),
        ])
        wv_full = np.ascontiguousarray(Wv.T.reshape(NKC, 128, D))
        wv_dev = np.ascontiguousarray(wv_full[:, :, hsel])
        bc_ = c // 4
        t0 = TOK * (c % 4)
        in_maps.append(dict(
            xT=xTf, xTo=np.ascontiguousarray(xTf[bc_][:, t0:t0 + TOK]),
            wqk=wqk_dev, wv=wv_dev, wo=wo_dev,
            w1a=w1a_dev, w1b=w1b_dev, w2=w2_dev,
            b1a=b1a_dev, b1b=b1b_dev, b2=b2i,
            gw=gw_dev, gb=gb_dev, nwc=nwc_dev, nwr=nwr_dev,
        ))
    return in_maps


def kernel(**inputs):
    in_maps = _prep(inputs)
    nc, _ = build_program()
    res = run_bass_kernel_spmd(nc, in_maps, core_ids=list(range(NCOR)))
    out = np.zeros((B, T, D), np.float32)
    for c in range(NCOR):
        bc_ = c // 4
        t0 = TOK * (c % 4)
        out[bc_, t0:t0 + TOK, :] = np.asarray(res.results[c]["outT"]).T
    return out
